# version 24
# baseline (speedup 1.0000x reference)
import numpy as np
from contextlib import ExitStack

import concourse.bass as bass
import concourse.mybir as mybir
from concourse.bass_utils import run_bass_kernel_spmd

F32 = mybir.dt.float32
BF16 = mybir.dt.bfloat16
I32 = mybir.dt.int32
AF = mybir.ActivationFunctionType
ALU = mybir.AluOpType
AX = mybir.AxisListType

NCORES = 8
B = 4
SL = 512
D = 1024
NH = 16
MOE_H = 4096
DFF = 4096
GI = 3072
NST = 16
EPS_LN = 1e-5
EPS_RMS = 1e-6
EPS_GATE = 1e-4
MOE_E = 8
DK = 64
UPTO = 99

ENGINES = ("sync", "gpsimd", "tensor", "vector", "scalar")



class Dep:
    __slots__ = ("w", "r")

    def __init__(self):
        self.w = None
        self.r = []

    def for_read(self):
        return [self.w]

    def for_write(self):
        return [self.w] + self.r

    def did_read(self, tok):
        if tok is not None:
            self.r.append(tok)

    def did_write(self, tok):
        self.w = tok
        self.r = []


class KB:
    SEMS = ("tS", "vS", "aS", "gS", "qS0", "qS1", "qG0", "qG1",
            "qA0", "qA1", "cc")

    def __init__(self, nc):
        self.nc = nc
        self.ops = {e: [] for e in ENGINES}
        self.cnt = {s: 0 for s in self.SEMS}
        self.bank_last_mm = {}
        self.bank_readers = {}
        self.sems = {}
        self.n = 0
        self._qn = {}

    def emit(self, eng, build, waits=(), inc=None):
        tok = None
        incs = []
        if inc is not None:
            name, n = inc if isinstance(inc, tuple) else (inc, 1)
            self.cnt[name] += n
            tok = (name, self.cnt[name])
            incs.append((name, n))
        ws = tuple(w for w in waits if w is not None)
        self.ops[eng].append((ws, build, incs))
        self.n += 1
        return tok

    @staticmethod
    def _collect(reads, writes, extra):
        w = list(extra)
        for d in reads:
            w += d.for_read()
        for d in writes:
            w += d.for_write()
        return w

    @staticmethod
    def _record(tok, reads, writes):
        for d in reads:
            d.did_read(tok)
        for d in writes:
            d.did_write(tok)

    def mm(self, ps_ap, lhsT, rhs, bank, start, stop, reads=(), waits=()):
        w = self._collect(reads, (), waits)
        w.append(self.bank_last_mm.get(bank))
        if start and bank in self.bank_readers:
            w += self.bank_readers.pop(bank)
        tok = self.emit(
            "tensor",
            lambda t, a=ps_ap, b=lhsT, c=rhs, s1=start, s2=stop: t.matmul(
                a, b, c, start=s1, stop=s2),
            w, "tS")
        self.bank_last_mm[bank] = tok
        self._record(tok, reads, ())
        return tok

    def transpose(self, ps_ap, in_ap, ident, bank, reads=(), waits=()):
        w = self._collect(reads, (), waits)
        w.append(self.bank_last_mm.get(bank))
        if bank in self.bank_readers:
            w += self.bank_readers.pop(bank)
        tok = self.emit(
            "tensor",
            lambda t, a=ps_ap, b=in_ap, c=ident: t.transpose(a, b, c),
            w, "tS")
        self.bank_last_mm[bank] = tok
        self._record(tok, reads, ())
        return tok

    def op(self, eng, name, *args, reads=(), writes=(), waits=(),
           reads_bank=None, allow_lp=False, **kw):
        sem = {"vector": "vS", "scalar": "aS", "gpsimd": "gS"}[eng]
        w = self._collect(reads, writes, waits)
        if reads_bank is not None:
            w.append(self.bank_last_mm.get(reads_bank))
        if allow_lp:
            nc_ = self.nc

            def bl(e, n=name, a=args, k=kw):
                with nc_.allow_low_precision(reason="bf16 matmul operand"):
                    return getattr(e, n)(*a, **k)
        else:
            bl = lambda e, n=name, a=args, k=kw: getattr(e, n)(*a, **k)
        bl._tag = name
        tok = self.emit(eng, bl, w, sem)
        if reads_bank is not None:
            self.bank_readers.setdefault(reads_bank, []).append(tok)
        self._record(tok, reads, writes)
        return tok

    def dma(self, queue, out, in_, reads=(), writes=(), waits=(), **kw):
        base = {"sync": "qS", "gpsimd": "qG", "scalar": "qA"}[queue]
        n = self._qn.get(queue, 0)
        self._qn[queue] = n + 1
        sem = f"{base}{n % 2}"
        w = self._collect(reads, writes, waits)
        w.append((sem, self.cnt[sem]) if self.cnt[sem] else None)
        tok = self.emit(
            queue,
            lambda e, o=out, i=in_, k=kw: e.dma_start(out=o, in_=i, **k),
            w, (sem, 16))
        self._record(tok, reads, writes)
        return tok

    def coll(self, kind, op, ins, outs, waits=()):
        return self.emit(
            "gpsimd",
            lambda g, k=kind, o=op, a=ins, b=outs: g.collective_compute(
                k, o, replica_groups=[list(range(NCORES))], ins=a, outs=b),
            waits, "cc")

    def wait(self, eng, waits):
        ws = tuple(w for w in waits if w is not None)
        if ws:
            self.ops[eng].append((ws, None, []))

    def barrier_tokens(self):
        return [(s, self.cnt[s]) for s in self.SEMS if self.cnt[s] > 0]

    def build_block(self, ctx):
        for s in self.SEMS:
            self.sems[s] = ctx.enter_context(self.nc.semaphore(s))
        block = ctx.enter_context(self.nc.Block())
        for eng in ENGINES:
            self._replay(block, eng)

    def _replay(self, block, eng):
        ops = self.ops[eng]
        sems = self.sems

        def run(engine):
            seen = {}
            for idx, (waits, build, incs) in enumerate(ops):
                todo = []
                for (s, v) in waits:
                    if seen.get(s, 0) < v:
                        todo.append((s, v))
                        seen[s] = v
                if build is None:
                    for s, v in todo:
                        engine.wait_ge(sems[s], v)
                    continue
                for s, v in todo[1:]:
                    engine.wait_ge(sems[s], v)
                try:
                    inst = build(engine)
                except Exception as ex:
                    raise RuntimeError(
                        f"emit failure on {eng} op#{idx}: "
                        f"{getattr(build, '_tag', '?')}: {ex}") from ex
                if todo:
                    s, v = todo[0]
                    inst._wait_ge(sems[s], v)
                for s, n in incs:
                    inst.then_inc(sems[s], n)

        getattr(block, eng)(run)


def mm_groups(K, groups):
    nk = max(len(g["chunks"]) for g in groups)
    for g in groups:
        g["_w"] = KB._collect(g.get("reads", ()), (), g.get("waits", ()))
        g["_t"] = None
    for ki in range(nk):
        for g in groups:
            if ki >= len(g["chunks"]):
                continue
            lhsT, rhs = g["chunks"][ki]
            start = (ki == 0) and g.get("first_start", True)
            stop = ki == len(g["chunks"]) - 1
            g["_t"] = K.mm(g["ps"], lhsT, rhs, g["bank"], start=start,
                           stop=stop, waits=g["_w"] if ki == 0 else ())
    for g in groups:
        for d in g.get("reads", ()):
            d.did_read(g["_t"])
        if g.get("evac"):
            g["evac"](K)



def build_program(upto=None, dbg=False):
    if upto is None:
        upto = UPTO
    S = SL * NCORES
    T = B * SL
    NC = D // 128
    TW = min(512, T)
    NTT = T // TW
    KTOT = S
    NKT = KTOT // 128
    NHP = NH // 2
    JJ = SL // 128

    nc = bass.Bass("TRN2", target_bir_lowering=False, debug=False)
    K = KB(nc)

    def din(name, shape, dt=F32):
        return nc.dram_tensor(name, shape, dt, kind="ExternalInput").ap()

    def dint(name, shape, dt=F32):
        return nc.dram_tensor(name, shape, dt, kind="Internal").ap()

    def sbuf(name, shape, dt):
        return nc.alloc_sbuf_tensor(name, shape, dt)

    xin = din("xin", [D, T])
    xout = nc.dram_tensor("xout", [D, T], F32, kind="ExternalOutput").ap()
    wqT = din("wqT", [D, D])
    wkT = din("wkT", [D, D])
    wvT = din("wvT", [D, D])
    woT = din("woT", [D, D])
    bqk = din("bqk", [4, D])

    agkv_in = dint("agkv_in", [2 * D * T], BF16)
    agkv_out = dint("agkv_out", [NCORES * 2 * D * T], BF16)
    agK_in = agkv_in[0:D * T].rearrange("(d t) -> d t", d=D)
    agV_in = agkv_in[D * T:2 * D * T].rearrange("(t d) -> t d", d=D)
    agK_out = agkv_out.rearrange("(r a d t) -> r a d t", r=NCORES, a=2, d=D)
    agV_out = agkv_out.rearrange("(r a t d) -> r a t d", r=NCORES, a=2, t=T)

    NKS_ = max(1, KTOT // 1024)
    KVW = B * (KTOT // NKS_)
    WACOLS = max(NC * 512, 2 * KVW)
    B2COLS = max(NC * T, (min(1024, MOE_H) // 128) * T)
    B1COLS = max(NC * T, (GI // 128) * TW, 12 * TW)
    ST = sbuf("ST", [128, NC * T], F32)
    B1 = sbuf("B1", [128, B1COLS], BF16)
    B2 = sbuf("B2", [128, B2COLS], BF16)
    F1 = sbuf("F1", [128, 8 * TW], F32)
    _wa_off = nc.sbuf_base
    WA = sbuf("WA", [128, WACOLS], BF16)
    SCN = nc.alloc_sbuf_tensor_at("SCN", [128, 3 * TW], F32,
                              offset=_wa_off)
    _wbu_off = nc.sbuf_base
    WBU = sbuf("WBU", [128, WACOLS], BF16)
    ITMP2 = nc.alloc_sbuf_tensor_at("ITMP2", [128, 512], I32,
                                    offset=_wbu_off)
    RW = sbuf("RW", [128, 3 * TW], F32)
    SM = sbuf("SM", [128, 688], F32)
    SMB = sbuf("SMB", [128, 128], BF16)
    ITMP = nc.alloc_sbuf_tensor("ITMP", [128, 512], I32)
    MSKS = sbuf("MSKS", [9, T], BF16)
    MSKB = sbuf("MSKB", [128, 1024], BF16)
    B2S = sbuf("B2S", [9, D], BF16)
    SEL = sbuf("SEL", [16, 16 * 128], BF16)
    XRT = sbuf("XRT", [NST, T], F32)

    dST = Dep()
    dB1 = Dep()
    dB2 = Dep()
    dB3 = Dep()
    dF1 = Dep()
    dWA = Dep()
    dWB = Dep()
    dRW = Dep()
    dSM = Dep()
    dSMB = Dep()
    dAGin = Dep()
    dAGout = Dep()
    dSEL = Dep()
    dMS = Dep()
    dMB = Dep()
    dB2S = Dep()

    PS = [nc.alloc_psum_tensor(f"ps{i}", [128, 512], F32) for i in range(8)]

    def STc(k):
        return ST[:, k * T:(k + 1) * T]

    def B1c(k):
        return B1[:, k * T:(k + 1) * T]

    def B2c(k):
        return B2[:, k * T:(k + 1) * T]

    IDENT = SM[:, 0:128]
    ONES = SM[:, 128:256]

    def bias_col(j, k):
        return SM[:, 256 + j * NC + k: 256 + j * NC + k + 1]

    def phase_barrier():
        toks = K.barrier_tokens()
        for e_ in ENGINES:
            K.wait(e_, toks)

    _consts = {}

    def constcol(val):
        if val not in _consts:
            idx = 664 + len(_consts)
            K.op("gpsimd", "memset", SM[:, idx:idx + 1], float(val),
                 writes=[dSM])
            _consts[val] = idx
        i = _consts[val]
        return SM[:, i:i + 1]

    K.dma("sync", out=ST[:].rearrange("p (c t) -> p c t", c=NC),
          in_=xin.rearrange("(c p) t -> p c t", p=128), writes=[dST])
    K.dma("sync", out=SM[:, 256:256 + 4 * NC].rearrange(
        "p (j k) -> p j k", j=4),
        in_=bqk.rearrange("j (k p) -> p j k", p=128), writes=[dSM],
        allow_slow_non_contiguous=True)

    t0 = K.op("gpsimd", "memset", SM[:, 0:128], 0.0, writes=[dSM])
    t0 = K.op("gpsimd", "affine_select", out=SM[:, 0:128], in_=SM[:, 0:128],
              compare_op=ALU.not_equal, fill=1.0, base=0,
              pattern=[[-1, 128]], channel_multiplier=1,
              waits=[t0], writes=[dSM])
    t1 = K.op("gpsimd", "memset", SMB[:, 0:128], 0.0, writes=[dSMB])
    t1 = K.op("gpsimd", "affine_select", out=SMB[:, 0:128], in_=SMB[:, 0:128],
              compare_op=ALU.not_equal, fill=1.0, base=0,
              pattern=[[-1, 128]], channel_multiplier=1,
              waits=[t1], writes=[dSMB])
    K.op("gpsimd", "memset", SM[:, 128:256], 1.0, writes=[dSM])

    def layernorm(src_chunk, src_dep, dst_chunk, dst_dep, gamma_col=None,
                  rms=False, eps=EPS_LN, tile_cb=None):
        last = None
        for tt in range(NTT):
            sl = slice(tt * TW, (tt + 1) * TW)
            gsum, gsq = [], []
            for k in range(NC):
                sq = F1[:, k * TW:(k + 1) * TW]
                K.op("scalar", "activation", sq, src_chunk(k)[:, sl],
                     AF.Square, reads=[src_dep], writes=[dF1])
                gsq.append((ONES[:, 0:1], sq))
                if not rms:
                    gsum.append((ONES[:, 0:1], src_chunk(k)[:, sl]))
            groups = [dict(bank=6, ps=PS[6][0:1, 0:TW], chunks=gsq,
                           reads=[dF1, dSM])]
            if not rms:
                groups.append(dict(bank=7, ps=PS[7][0:1, 0:TW], chunks=gsum,
                                   reads=[src_dep, dSM]))
            mm_groups(K, groups)
            mrow = RW[0:1, 0:TW]
            rrow = RW[0:1, TW:2 * TW]
            trow = RW[0:1, 2 * TW:3 * TW]
            if not rms:
                K.op("vector", "tensor_scalar", mrow, PS[7][0:1, 0:TW],
                     1.0 / D, None, ALU.mult, reads_bank=7, writes=[dRW])
                K.op("vector", "tensor_scalar", rrow, PS[6][0:1, 0:TW],
                     1.0 / D, None, ALU.mult, reads_bank=6, writes=[dRW])
                K.op("vector", "tensor_tensor", trow, mrow, mrow,
                     ALU.mult, reads=[dRW], writes=[dRW])
                K.op("vector", "tensor_tensor", rrow, rrow, trow,
                     ALU.subtract, reads=[dRW], writes=[dRW])
            else:
                K.op("vector", "tensor_scalar", rrow, PS[6][0:1, 0:TW],
                     1.0 / D, None, ALU.mult, reads_bank=6, writes=[dRW])
            K.op("scalar", "activation", rrow, rrow, AF.Ln,
                 bias=constcol(eps)[0:1, :], reads=[dRW, dSM], writes=[dRW])
            K.op("scalar", "activation", rrow, rrow, AF.Exp,
                 scale=-0.5, reads=[dRW], writes=[dRW])
            K.mm(PS[5][:, 0:TW], ONES[0:1, 0:128], rrow, 5,
                 start=True, stop=True, reads=[dRW, dSM])
            if not rms:
                K.mm(PS[4][:, 0:TW], ONES[0:1, 0:128], mrow, 4,
                     start=True, stop=True, reads=[dRW, dSM])
            if tile_cb is not None:
                tile_cb(tt)
            for k in range(NC):
                tmp = F1[:, (k % 2) * TW:((k % 2) + 1) * TW]
                if not rms:
                    K.op("vector", "tensor_tensor", tmp,
                         src_chunk(k)[:, sl], PS[4][:, 0:TW], ALU.subtract,
                         reads=[src_dep], writes=[dF1], reads_bank=4)
                    last = K.op("vector", "tensor_tensor",
                                dst_chunk(k)[:, sl], tmp, PS[5][:, 0:TW],
                                ALU.mult, reads=[dF1], writes=[dst_dep],
                                reads_bank=5)
                else:
                    K.op("vector", "tensor_tensor", tmp,
                         src_chunk(k)[:, sl], PS[5][:, 0:TW], ALU.mult,
                         reads=[src_dep], writes=[dF1], reads_bank=5)
                    last = K.op("vector", "tensor_scalar",
                                dst_chunk(k)[:, sl], tmp, gamma_col(k), None,
                                ALU.mult, reads=[dF1, dSM],
                                writes=[dst_dep])
        return last

    def dense_fm(wT_dram, n_out, src_chunk, src_dep, evac_fn):
        wb = [WA, WBU]
        wdep = [dWA, dWB]
        HWO = min(512, n_out)
        nhalf = n_out // HWO
        pend = []

        def flush():
            while pend:
                pair, pend[:] = pend[:2], pend[2:]
                mm_groups(K, pair)

        for half in range(nhalf):
            buf = wb[half % 2]
            bd = wdep[half % 2]
            ldt = K.dma("gpsimd",
                        out=buf[:, 0:NC * HWO]
                        .rearrange("p (c o) -> p c o", c=NC),
                        in_=wT_dram[:, half * HWO:(half + 1) * HWO]
                        .rearrange("(c p) o -> p c o", p=128),
                        writes=[bd])
            for oc in range(HWO // 128):
                o_glob = half * (HWO // 128) + oc
                for tt in range(NTT):
                    bank = (o_glob * NTT + tt) % 2
                    chunks = []
                    for k in range(NC):
                        lhsT = buf[:, k * HWO + oc * 128:
                                   k * HWO + oc * 128 + 128]
                        rhs = src_chunk(k)[:, tt * TW:(tt + 1) * TW]
                        chunks.append((lhsT, rhs))
                    pend.append(dict(
                        bank=bank, ps=PS[bank][:, 0:TW], chunks=chunks,
                        reads=[bd, src_dep], waits=[ldt],
                        evac=(lambda KK, o=o_glob, t_=tt, b_=bank:
                              evac_fn(KK, o, t_, b_, PS[b_][:, 0:TW]))))
                    if len(pend) == 2:
                        flush()
        flush()

    def evac_act_bias(dst_chunk, dst_dep, bias_j, act=AF.Identity):
        def f(KK, o, tt, bank, ps):
            KK.op("scalar", "activation",
                  dst_chunk(o)[:, tt * TW:(tt + 1) * TW], ps, act,
                  bias=bias_col(bias_j, o), reads=[dSM], writes=[dst_dep],
                  reads_bank=bank)
        return f

    layernorm(STc, dST, B1c, dB1)

    if upto >= 2:
        dense_fm(wkT, D, B1c, dB1, evac_act_bias(B2c, dB2, 1))
        stK = K.dma("sync", out=agK_in.rearrange("(c p) t -> p c t", p=128),
                    in_=B2[:, 0:NC * T].rearrange("p (c t) -> p c t", c=NC),
                    reads=[dB2], writes=[dAGin])
        v_toks = []

        def evac_v(KK, o, tt, bank, ps):
            fsl = F1[:, 4 * TW:5 * TW]
            KK.op("scalar", "activation", fsl, ps, AF.Identity,
                  bias=bias_col(2, o), reads=[dSM], writes=[dF1],
                  reads_bank=bank)
            for jj_ in range(TW // 128):
                tb = 2 + (jj_ % 2)
                KK.transpose(PS[tb][:, 0:128],
                             fsl[:, jj_ * 128:(jj_ + 1) * 128],
                             IDENT, tb, reads=[dF1, dSM])
                KK.op("vector", "tensor_copy",
                      MSKB[:, jj_ * 128:(jj_ + 1) * 128],
                      PS[tb][:, 0:128], reads_bank=tb, writes=[dMB])
            v_toks.append(KK.dma(
                "gpsimd",
                out=agV_in[tt * TW:(tt + 1) * TW, o * 128:(o + 1) * 128]
                .rearrange("(jj q) e -> q jj e", q=128),
                in_=MSKB[:, 0:TW].rearrange("p (jj e) -> p jj e", e=128),
                reads=[dMB], writes=[dAGin]))

        dense_fm(wvT, D, B1c, dB1, evac_v)
        stV = v_toks[-1]
        ag_tok = K.coll("AllGather", ALU.bypass, [agkv_in.opt()],
                        [agkv_out.opt()], waits=[stK, stV] + dAGout.for_write())
        dAGout.did_write(ag_tok)
        dense_fm(wqT, D, B1c, dB1, evac_act_bias(B2c, dB2, 0))

    def attention():
        NKS = max(1, KTOT // 512)
        KSW = KTOT // NKS
        RPS = max(1, KSW // SL)
        NJ = KSW // 128
        KHALF = B * KSW
        NHALF = max(1, min(2, KVW // KHALF))
        dKh = [Dep(), Dep()]
        dVh = [Dep(), Dep()]
        dHA = Dep()
        WAK = WACOLS - KVW
        HA = F1[:, 4 * TW:4 * TW + T]
        dPLp = [Dep(), Dep()]
        dMCp = [Dep(), Dep()]
        dZCp = [Dep(), Dep()]
        dATp = [Dep(), Dep()]
        assert WAK >= 8 * SL, (WAK, SL)
        assert NHALF * KHALF <= KVW
        _bt = K.barrier_tokens()
        for _d in dPLp + dMCp + dZCp + dATp + dKh + dVh:
            _d.r = list(_bt)
        blocks = [(hp, ks) for hp in range(NHP) for ks in range(NKS)]
        blk = {}

        def mkloads(bi):
            hp, ks = blocks[bi]
            kp = ks % NHALF
            KOFF = WAK + kp * KHALF
            ldk, ldv = [], []
            for b_ in range(B):
                ldk.append(K.dma(
                    "sync",
                    out=WA[:, KOFF + b_ * KSW:KOFF + (b_ + 1) * KSW]
                    .rearrange("p (rr s) -> p rr s", rr=RPS),
                    in_=agK_out[ks * RPS:(ks + 1) * RPS, 0,
                                hp * 128:(hp + 1) * 128,
                                b_ * SL:(b_ + 1) * SL]
                    .rearrange("r d s -> d r s"),
                    writes=[dKh[kp]], reads=[dAGout]))
                vtoks = []
                for rr in range(RPS):
                    vtoks.append(K.dma(
                        "sync",
                        out=WBU[:, KOFF + b_ * KSW + rr * JJ * 128:
                                KOFF + b_ * KSW + (rr + 1) * JJ * 128]
                        .rearrange("p (jj e) -> p jj e", jj=JJ),
                        in_=agV_out[ks * RPS + rr, 1,
                                    b_ * SL:(b_ + 1) * SL,
                                    hp * 128:(hp + 1) * 128]
                        .rearrange("(jj q) e -> q jj e", jj=JJ),
                        writes=[dVh[kp]], reads=[dAGout]))
                ldv.append(vtoks[-1])
            blk[bi] = (kp, KOFF, ldk, ldv)

        def s1mm(bi, h2, j, up):
            hp, ks = blocks[bi]
            kp, KOFF, ldk, ldv = blk[bi]
            prow = slice(h2 * 64, h2 * 64 + 64)
            for b_ in range(B):
                lhsT = WA[prow, KOFF + b_ * KSW + j * 128:
                          KOFF + b_ * KSW + j * 128 + 128]
                rhs = B2[prow, hp * T + b_ * SL:
                         hp * T + (b_ + 1) * SL]
                K.mm(PS[4 + b_][:, 0:SL], lhsT, rhs, 4 + b_,
                     start=True, stop=True,
                     reads=[dKh[kp], dB2], waits=[ldk[b_]])

        def s1ev(bi, h2, j, up):
            PL = WA[:, up * 4 * SL:up * 4 * SL + 4 * SL]
            for b_ in range(B):
                K.op("vector", "tensor_copy",
                     PL[:, b_ * SL:(b_ + 1) * SL],
                     PS[4 + b_][:, 0:SL],
                     reads_bank=4 + b_, writes=[dPLp[up]])

        def s2a(bi, h2, j, up):
            PL = WA[:, up * 4 * SL:up * 4 * SL + 4 * SL]
            MC = MSKB[:, up * SL:(up + 1) * SL]
            AT = WBU[:, up * 4 * SL:up * 4 * SL + 4 * SL]
            dPL = dPLp[up]
            dMC = dMCp[up]
            dAT = dATp[up]
            K.op("vector", "tensor_tensor", MC,
                 PL[:, 0:SL], PL[:, SL:2 * SL], ALU.max,
                 reads=[dPL], writes=[dMC])
            K.op("vector", "tensor_tensor", MC,
                 MC, PL[:, 2 * SL:3 * SL], ALU.max,
                 reads=[dPL, dMC], writes=[dMC])
            K.op("vector", "tensor_tensor", MC,
                 MC, PL[:, 3 * SL:4 * SL], ALU.max,
                 reads=[dPL, dMC], writes=[dMC])
            K.op("vector", "tensor_tensor", PL, PL,
                 MC.unsqueeze(1).broadcast_to([128, 4, SL]),
                 ALU.subtract,
                 reads=[dMC, dPL], writes=[dPL])
            K.op("scalar", "activation", AT, PL, AF.Exp,
                 reads=[dPL], writes=[dAT])

        def s2b(bi, h2, j, up):
            hp, ks = blocks[bi]
            kp, KOFF, ldk, ldv = blk[bi]
            prow = slice(h2 * 64, h2 * 64 + 64)
            PL = WA[:, up * 4 * SL:up * 4 * SL + 4 * SL]
            MC = MSKB[:, up * SL:(up + 1) * SL]
            ZC = RW[:, up * SL:(up + 1) * SL]
            AT = WBU[:, up * 4 * SL:up * 4 * SL + 4 * SL]
            dPL = dPLp[up]
            dMC = dMCp[up]
            dZC = dZCp[up]
            dAT = dATp[up]
            K.op("vector", "tensor_tensor", ZC,
                 AT[:, 0:SL], AT[:, SL:2 * SL], ALU.add,
                 reads=[dAT], writes=[dZC])
            K.op("vector", "tensor_tensor", ZC, ZC,
                 AT[:, 2 * SL:3 * SL], ALU.add,
                 reads=[dAT, dZC], writes=[dZC])
            K.op("vector", "tensor_tensor", ZC, ZC,
                 AT[:, 3 * SL:4 * SL], ALU.add,
                 reads=[dAT, dZC], writes=[dZC])
            K.op("scalar", "activation", MC, ZC, AF.Ln,
                 reads=[dZC], writes=[dMC])
            K.op("vector", "tensor_tensor", PL, PL,
                 MC.unsqueeze(1).broadcast_to([128, 4, SL]),
                 ALU.subtract,
                 reads=[dMC, dPL], writes=[dPL])
            ta = K.op("scalar", "activation", AT, PL, AF.Exp,
                      reads=[dPL], writes=[dAT])
            for b_ in range(B):
                off = (KOFF + b_ * KSW + (j // JJ) * (JJ * 128)
                       + (j % JJ) * 128 + h2 * 64)
                lhsT = WBU[:, off:off + 64]
                K.mm(PS[b_][prow, 0:SL], lhsT,
                     AT[:, b_ * SL:(b_ + 1) * SL], b_,
                     start=(j == 0), stop=(j == NJ - 1),
                     reads=[dVh[kp], dATp[up]],
                     waits=[ldv[b_], ta])

        def fold(bi):
            hp, ks = blocks[bi]
            for h2 in range(2):
                prow = slice(h2 * 64, h2 * 64 + 64)
                for b_ in range(B):
                    dst = HA[prow, b_ * SL:(b_ + 1) * SL]
                    if ks == 0:
                        K.op("vector", "tensor_copy", dst,
                             PS[b_][prow, 0:SL], reads_bank=b_,
                             writes=[dHA])
                    else:
                        K.op("vector", "tensor_tensor", dst, dst,
                             PS[b_][prow, 0:SL], ALU.add,
                             reads=[dHA], reads_bank=b_, writes=[dHA])
            if ks == NKS - 1:
                K.op("vector", "tensor_copy", B1c(hp), HA[:, 0:T],
                     reads=[dHA], writes=[dB1])

        NUB = 2 * NJ
        if NHALF == 1:
            ub = [(h2, j) for h2 in range(2) for j in range(NJ)]
            for bi in range(len(blocks)):
                mkloads(bi)
                for ui in range(NUB):
                    h2, j = ub[ui]
                    s1mm(bi, h2, j, ui % 2)
                    if ui >= 1:
                        s2a(bi, ub[ui - 1][0], ub[ui - 1][1],
                            (ui - 1) % 2)
                    if ui >= 2:
                        s2b(bi, ub[ui - 2][0], ub[ui - 2][1],
                            (ui - 2) % 2)
                    s1ev(bi, h2, j, ui % 2)
                s2a(bi, ub[NUB - 1][0], ub[NUB - 1][1], (NUB - 1) % 2)
                if NUB >= 2:
                    s2b(bi, ub[NUB - 2][0], ub[NUB - 2][1],
                        (NUB - 2) % 2)
                s2b(bi, ub[NUB - 1][0], ub[NUB - 1][1], (NUB - 1) % 2)
                fold(bi)
            allu = []
            ng = 0
        else:
            allu = [(bi, h2, j) for bi in range(len(blocks))
                    for h2 in range(2) for j in range(NJ)]
            ng = len(allu)
            for bi0 in range(min(NHALF, len(blocks))):
                mkloads(bi0)
        for gi in range(ng):
            bi, h2, j = allu[gi]
            s1mm(bi, h2, j, gi % 2)
            if gi >= 1:
                p1 = allu[gi - 1]
                s2a(p1[0], p1[1], p1[2], (gi - 1) % 2)
            if gi >= 2:
                p2 = allu[gi - 2]
                s2b(p2[0], p2[1], p2[2], (gi - 2) % 2)
                if p2[1] == 1 and p2[2] == NJ - 1:
                    fold(p2[0])
                    if p2[0] + NHALF < len(blocks):
                        mkloads(p2[0] + NHALF)
            s1ev(bi, h2, j, gi % 2)
        if ng:
            p1 = allu[ng - 1]
            s2a(p1[0], p1[1], p1[2], (ng - 1) % 2)
            for gi in (ng - 2, ng - 1):
                p2 = allu[gi]
                s2b(p2[0], p2[1], p2[2], gi % 2)
                if p2[1] == 1 and p2[2] == NJ - 1:
                    fold(p2[0])
                    if p2[0] + NHALF < len(blocks):
                        mkloads(p2[0] + NHALF)

    if upto >= 3:
        attention()
        phase_barrier()

        def evac_wo(KK, o, tt, bank, ps):
            KK.op("vector", "scalar_tensor_tensor",
                  STc(o)[:, tt * TW:(tt + 1) * TW],
                  ps, bias_col(3, o),
                  STc(o)[:, tt * TW:(tt + 1) * TW],
                  ALU.add, ALU.add,
                  reads=[dSM, dST], writes=[dST], reads_bank=bank)

        dense_fm(woT, D, B1c, dB1, evac_wo)

    if upto >= 4:
        HSW = min(1024, MOE_H)
        NHS = MOE_H // HSW
        HCS = HSW // 128
        NT128 = T // 128
        moe_w1 = din("moe_w1", [MOE_E, D, MOE_H])
        moe_w2 = din("moe_w2", [MOE_E, MOE_H, D])
        moe_b1 = din("moe_b1", [MOE_E, MOE_H])
        moe_b2 = din("moe_b2", [MOE_E, D])
        wgrT = din("wgrT", [D, NST])
        wg_in = din("wg", [MOE_E, NST])
        st_spill = dint("st_spill", [D, T])
        dSP = Dep()

        phase_barrier()
        WGR32 = SM[:, 532:532 + NC * NST]
        ld_wgr32 = K.dma("sync", out=WGR32.rearrange("p (c g) -> p c g",
                                                     c=NC),
                         in_=wgrT.rearrange("(c p) g -> p c g", p=128),
                         writes=[dSM])
        WGS = SM[0:NST, 676:677]
        gw = [(WGR32[:, k * NST:(k + 1) * NST], ONES[:, 0:1])
              for k in range(NC)]
        mm_groups(K, [dict(bank=3, ps=PS[3][0:NST, 0:1], chunks=gw,
                           reads=[dSM], waits=[ld_wgr32])])
        K.op("vector", "tensor_copy", WGS, PS[3][0:NST, 0:1], reads_bank=3,
             writes=[dSM])
        XR = XRT[:, :]
        dXR = Dep()

        def gate_cb(tt):
            sl_ = slice(tt * TW, (tt + 1) * TW)
            chunks = [(WGR32[:, k * NST:(k + 1) * NST], STc(k)[:, sl_])
                      for k in range(NC)]
            mm_groups(K, [dict(bank=3, ps=PS[3][0:NST, 0:TW], chunks=chunks,
                               reads=[dSM, dST])])
            tmp = RW[0:NST, 0:TW]
            K.op("vector", "tensor_scalar", tmp, PS[4][0:NST, 0:TW], WGS,
                 None, ALU.mult, reads=[dSM], reads_bank=4, writes=[dRW])
            K.op("vector", "tensor_tensor", tmp, PS[3][0:NST, 0:TW], tmp,
                 ALU.subtract, reads=[dRW], writes=[dRW], reads_bank=3)
            K.op("vector", "tensor_tensor", XR[:, sl_], tmp,
                 PS[5][0:NST, 0:TW], ALU.mult, reads=[dRW], writes=[dXR],
                 reads_bank=5)

        tsel = K.op("gpsimd", "memset", SEL[:, :], 1.0, writes=[dSEL])
        tsel = K.op("gpsimd", "affine_select", out=SEL[:, :], in_=SEL[:, :],
                    compare_op=ALU.is_ge, fill=0.0, base=0,
                    channel_multiplier=-128, pattern=[[1, 16 * 128]],
                    waits=[tsel], writes=[dSEL])
        tsel = K.op("gpsimd", "affine_select", out=SEL[:, :], in_=SEL[:, :],
                    compare_op=ALU.is_ge, fill=0.0, base=127,
                    channel_multiplier=128, pattern=[[-1, 16 * 128]],
                    waits=[tsel], writes=[dSEL])
        layernorm(STc, dST, B1c, dB1, tile_cb=gate_cb)
        tsp = K.dma("sync", out=st_spill.rearrange("(c p) t -> p c t", p=128),
                    in_=ST[:].rearrange("p (c t) -> p c t", c=NC),
                    reads=[dST], writes=[dSP])
        tz = K.op("vector", "memset", ST[:], 0.0, waits=[tsp], writes=[dST])

        ld_wg = K.dma("sync", out=SM[0:MOE_E, 544:544 + NST], in_=wg_in,
                      writes=[dSM])
        ld_b1 = K.dma("sync", out=SM[:, 288:288 + MOE_E * (MOE_H // 128)]
                      .rearrange("p (e k) -> p e k", e=MOE_E),
                      in_=moe_b1.rearrange("e (k p) -> p e k", p=128),
                      writes=[dSM], allow_slow_non_contiguous=True)
        ld_b2 = K.dma("gpsimd", out=B2S[1:9, :], in_=moe_b2, writes=[dB2S])
        ld_b2 = K.op("gpsimd", "memset", B2S[0:1, :], 0.0, waits=[ld_b2],
                     writes=[dB2S])

        def b1_col(e, hk):
            c = 288 + e * (MOE_H // 128) + hk
            return SM[:, c:c + 1]

        WGB = SM[0:MOE_E, 544:544 + NST]
        WGU = SM[0:MOE_E, 560:560 + NST]
        WGN = SM[0:MOE_E, 576:577]
        K.op("scalar", "activation", WGU, WGB, AF.Square,
             reads=[dSM], writes=[dSM])
        K.op("vector", "tensor_reduce", WGN, WGU, AX.X, ALU.add,
             reads=[dSM], writes=[dSM])
        K.op("scalar", "activation", WGN, WGN, AF.Ln, reads=[dSM],
             writes=[dSM])
        K.op("scalar", "activation", WGN, WGN, AF.Exp, scale=0.5,
             reads=[dSM], writes=[dSM])
        K.op("vector", "tensor_scalar", WGN, WGN, float(EPS_GATE), None,
             ALU.max, reads=[dSM], writes=[dSM])
        K.op("vector", "reciprocal", WGN, WGN, reads=[dSM], writes=[dSM])
        K.op("vector", "tensor_scalar", WGU, WGB, WGN, None, ALU.mult,
             reads=[dSM], writes=[dSM])
        WGUT = SM[0:NST, 600:600 + MOE_E]
        K.transpose(PS[5][0:NST, 0:MOE_E], WGU, IDENT[0:MOE_E, 0:MOE_E], 5,
                    reads=[dSM])
        K.op("vector", "tensor_copy", WGUT, PS[5][0:NST, 0:MOE_E],
             reads_bank=5, writes=[dSM])
        GE = SM[:, 608:608 + MOE_E]
        NM = SM[:, 616:617]
        ZS = SM[:, 617:618]
        IDX = SM[:, 619:620]
        SCC = SM[:, 620:621]
        MK8 = SM[:, 621:621 + MOE_E]
        MK9 = SM[:, 620:629]
        IOTA8 = SM[:, 650:650 + MOE_E]
        K.op("gpsimd", "iota", IOTA8, pattern=[[1, MOE_E]], base=0,
             channel_multiplier=0, allow_small_or_imprecise_dtypes=True,
             writes=[dSM])
        for jt in range(NT128):
            bk = 4 + jt % 2
            K.mm(PS[bk][0:128, 0:MOE_E], XR[:, jt * 128:(jt + 1) * 128],
                 WGUT, bk, start=True, stop=True, reads=[dXR, dSM])
            MX = SM[:, 632:633]
            K.op("vector", "tensor_reduce", MX, PS[bk][0:128, 0:MOE_E],
                 AX.X, ALU.max, reads_bank=bk, writes=[dSM])
            K.op("vector", "tensor_scalar", NM, MX, -1.0, None, ALU.mult,
                 reads=[dSM], writes=[dSM])
            K.op("scalar", "activation", GE, PS[bk][0:128, 0:MOE_E],
                 AF.Exp, bias=NM, reads=[dSM], reads_bank=bk, writes=[dSM])
            K.op("vector", "tensor_reduce", ZS, GE, AX.X, ALU.add,
                 reads=[dSM], writes=[dSM])
            K.op("vector", "reciprocal", SCC, ZS, reads=[dSM], writes=[dSM])
            EQ = SM[:, 633:633 + MOE_E]
            K.op("vector", "tensor_scalar", EQ, PS[bk][0:128, 0:MOE_E],
                 MX, None, ALU.is_equal, reads=[dSM], reads_bank=bk,
                 writes=[dSM])
            K.op("vector", "tensor_scalar", EQ, EQ, -1e9, 1e9, ALU.mult,
                 ALU.add, reads=[dSM], writes=[dSM])
            K.op("vector", "tensor_tensor", EQ, EQ, IOTA8, ALU.add,
                 reads=[dSM], writes=[dSM])
            K.op("vector", "tensor_reduce", IDX, EQ, AX.X, ALU.min,
                 reads=[dSM], writes=[dSM])
            K.op("vector", "tensor_scalar", MK8, IOTA8, IDX, None,
                 ALU.is_equal, reads=[dSM], writes=[dSM])
            K.transpose(PS[2][0:9, 0:128], MK9, IDENT, 2, reads=[dSM])
            K.op("vector", "tensor_copy",
                 MSKS[0:9, jt * 128:(jt + 1) * 128],
                 PS[2][0:9, 0:128], reads_bank=2, writes=[dMS])
            K.op("vector", "tensor_copy",
                 XRT[0:1, jt * 128:(jt + 1) * 128],
                 PS[2][0:1, 0:128], reads_bank=2, writes=[dXR])

        for e in range(MOE_E):
            for hs in range(NHS):
                wbuf, wdep = (WA, dWA) if hs % 2 == 0 else (WBU, dWB)
                hreg = 0
                ldw1 = K.dma("gpsimd",
                             out=wbuf[:, 0:NC * HSW]
                             .rearrange("p (c h) -> p c h", c=NC),
                             in_=moe_w1[e, :, hs * HSW:(hs + 1) * HSW]
                             .rearrange("(c p) h -> p c h", p=128),
                             writes=[wdep])
                pend = []
                for hc in range(HCS):
                    for tt in range(NTT):
                        bank = (hc * NTT + tt) % 4
                        chunks = [(wbuf[:, k * HSW + hc * 128:
                                        k * HSW + hc * 128 + 128],
                                   B1c(k)[:, tt * TW:(tt + 1) * TW])
                                  for k in range(NC)]

                        def ev_h(KK, hc=hc, tt=tt, bank=bank, hs=hs,
                                 hreg=hreg, e=e):
                            KK.op("scalar", "activation",
                                  B2[:, hreg + hc * T + tt * TW:
                                     hreg + hc * T + (tt + 1) * TW],
                                  PS[bank][:, 0:TW], AF.Gelu,
                                  bias=b1_col(e, hs * HCS + hc),
                                  reads=[dSM], writes=[dB2],
                                  reads_bank=bank)
                        pend.append(dict(bank=bank, ps=PS[bank][:, 0:TW],
                                         chunks=chunks, reads=[wdep, dB1],
                                         waits=[ldw1], evac=ev_h))
                        if len(pend) == 4:
                            mm_groups(K, pend)
                            pend = []
                if pend:
                    mm_groups(K, pend)
                wb2, wd2 = (WBU, dWB) if hs % 2 == 0 else (WA, dWA)
                ldw2 = K.dma("gpsimd",
                             out=wb2[:, 0:HCS * D]
                             .rearrange("p (c o) -> p c o", c=HCS),
                             in_=moe_w2[e, hs * HSW:(hs + 1) * HSW, :]
                             .rearrange("(c p) o -> p c o", p=128),
                             writes=[wd2])
                pend = []
                for tt in range(NTT):
                    bk = 6 + tt % 2
                    K.mm(PS[bk][:, 0:TW],
                         SEL[0:9, (e + 1) * 128:(e + 2) * 128],
                         MSKS[0:9, tt * TW:(tt + 1) * TW], bk,
                         start=True, stop=True, reads=[dMS, dSEL])
                    K.op("vector", "tensor_copy", MSKB[:, 0:TW],
                         PS[bk][:, 0:TW], reads_bank=bk, writes=[dMB])
                    for oc in range(NC):
                        bank = 4 + (oc + tt) % 2
                        chunks = [(wb2[:, c * D + oc * 128:
                                       c * D + oc * 128 + 128],
                                   B2[:, hreg + c * T + tt * TW:
                                      hreg + c * T + (tt + 1) * TW])
                                  for c in range(HCS)]

                        def ev_o(KK, oc=oc, tt=tt, bank=bank):
                            tmp = F1[:, 0:TW]
                            KK.op("vector", "tensor_tensor", tmp,
                                  PS[bank][:, 0:TW],
                                  MSKB[:, 0:TW], ALU.mult,
                                  reads=[dMB], writes=[dF1],
                                  reads_bank=bank)
                            KK.op("vector", "tensor_tensor",
                                  STc(oc)[:, tt * TW:(tt + 1) * TW],
                                  STc(oc)[:, tt * TW:(tt + 1) * TW],
                                  tmp, ALU.add, reads=[dF1, dST],
                                  writes=[dST])
                        pend.append(dict(bank=bank, ps=PS[bank][:, 0:TW],
                                         chunks=chunks, reads=[wd2, dB2],
                                         waits=[ldw2], evac=ev_o))
                        if len(pend) == 2:
                            mm_groups(K, pend)
                            pend = []
                    if pend:
                        mm_groups(K, pend)
                        pend = []

        for tt in range(NTT):
            bk = 6 + tt % 2
            K.mm(PS[bk][:, 0:TW], ONES[0:1, 0:128],
                 XRT[0:1, tt * TW:(tt + 1) * TW], bk, start=True, stop=True,
                 reads=[dXR, dSM])
            K.op("vector", "tensor_copy", MSKB[:, 0:TW],
                 PS[bk][:, 0:TW], reads_bank=bk, writes=[dMB])
            lds = K.dma("sync",
                        out=F1[:, 0:NC * TW]
                        .rearrange("p (c w) -> p c w", c=NC),
                        in_=st_spill.rearrange("(c p) t -> p c t", p=128)
                        [:, :, tt * TW:(tt + 1) * TW],
                        reads=[dSP], writes=[dF1])
            for oc in range(NC):
                bank = oc % 4
                K.mm(PS[bank][:, 0:TW],
                     B2S[:, oc * 128:(oc + 1) * 128].transpose([0, 1])
                     if False else B2S[0:9, oc * 128:(oc + 1) * 128],
                     MSKS[0:9, tt * TW:(tt + 1) * TW], bank,
                     start=True, stop=True, reads=[dB2S, dMS],
                     waits=[ld_b2])
                sl_ = slice(tt * TW, (tt + 1) * TW)
                t1 = K.op("vector", "tensor_tensor", PS[bank][:, 0:TW]
                          if False else STc(oc)[:, sl_],
                          STc(oc)[:, sl_], PS[bank][:, 0:TW], ALU.add,
                          reads=[dST], writes=[dST], reads_bank=bank)
                t2 = K.op("vector", "tensor_tensor", STc(oc)[:, sl_],
                          STc(oc)[:, sl_], MSKB[:, 0:TW], ALU.mult,
                          reads=[dST, dMB], writes=[dST])
                t3 = K.op("vector", "tensor_tensor", STc(oc)[:, sl_],
                          STc(oc)[:, sl_],
                          F1[:, oc * TW:(oc + 1) * TW], ALU.add,
                          reads=[dST, dF1], writes=[dST], waits=[lds])

    if upto >= 5:
        QW = min(512, SL)
        NQ = SL // QW
        wdtT = din("wdtT", [D, D])
        wBT = din("wBT", [D, NST])
        wCT = din("wCT", [D, NST])
        w1lT = din("w1lT", [D, DFF])
        w2lT = din("w2lT", [DFF, D])
        l1vec = din("l1vec", [6, D])
        l1b1 = din("l1b1", [DFF])
        l1bBC = din("l1bBC", [2, NST])
        l1A = din("l1A", [NST])
        cmask_in = din("cmask", [8])
        y_spill = dint("y_spill", [D, T])
        SUMW = NST * NC
        agm_in = dint("agm_in", [B, 128, 2 * SUMW])
        agm_out = dint("agm_out", [NCORES, B, 128, 2 * SUMW])
        dYS = Dep()
        dAGM = Dep()

        phase_barrier()
        K.dma("sync", out=SM[:, 256:256 + 6 * NC]
              .rearrange("p (j k) -> p j k", j=6),
              in_=l1vec.rearrange("j (k p) -> p j k", p=128),
              writes=[dSM], allow_slow_non_contiguous=True)
        K.dma("sync", out=SM[:, 304:304 + DFF // 128],
              in_=l1b1.rearrange("(k p) -> p k", p=128), writes=[dSM],
              allow_slow_non_contiguous=True)
        K.dma("sync", out=SM[0:NST, 336:338],
              in_=l1bBC.rearrange("j g -> g j"), writes=[dSM],
              allow_slow_non_contiguous=True)
        K.dma("sync", out=SM[0:1, 338:338 + NST],
              in_=l1A.rearrange("(j g) -> j g", j=1), writes=[dSM],
              allow_slow_non_contiguous=False)
        K.dma("sync", out=SM[0:1, 388:396],
              in_=cmask_in.rearrange("(j g) -> j g", j=1), writes=[dSM])

        def l1col(j, k):
            return SM[:, 256 + j * NC + k:256 + j * NC + k + 1]

        def b1lcol(hk):
            return SM[:, 304 + hk:304 + hk + 1]

        K.op("scalar", "activation", SM[0:1, 338:338 + NST],
             SM[0:1, 338:338 + NST], AF.Exp, reads=[dSM], writes=[dSM])
        K.op("vector", "tensor_scalar", SM[0:1, 338:338 + NST],
             SM[0:1, 338:338 + NST], -1.0, None, ALU.mult,
             reads=[dSM], writes=[dSM])
        K.mm(PS[2][:, 0:NST], ONES[0:1, 0:128], SM[0:1, 338:338 + NST], 2,
             start=True, stop=True, reads=[dSM])
        K.op("vector", "tensor_copy", SM[:, 354:354 + NST], PS[2][:, 0:NST],
             reads_bank=2, writes=[dSM])
        K.mm(PS[3][:, 0:8], ONES[0:1, 0:128], SM[0:1, 388:396], 3,
             start=True, stop=True, reads=[dSM])
        K.op("vector", "tensor_copy", SM[:, 396:404], PS[3][:, 0:8],
             reads_bank=3, writes=[dSM])

        def acol(n):
            return SM[:, 354 + n:354 + n + 1]

        layernorm(STc, dST, B1c, dB1, gamma_col=lambda k: l1col(0, k),
                  rms=True, eps=EPS_RMS)
        tsp = K.dma("sync", out=st_spill.rearrange("(c p) t -> p c t", p=128),
                    in_=ST[:].rearrange("p (c t) -> p c t", c=NC),
                    reads=[dST], writes=[dSP])
        K.wait("sync", [tsp])

        C1 = constcol(1.0)

        def evac_softplus(KK, o, tt, bank, ps):
            X = RW[:, 0:TW]
            Yt = RW[:, TW:2 * TW]
            KK.op("scalar", "activation", X, ps, AF.Abs,
                  bias=l1col(4, o), reads=[dSM], writes=[dRW],
                  reads_bank=bank)
            KK.op("scalar", "activation", Yt, X, AF.Exp, scale=-1.0,
                  reads=[dRW], writes=[dRW])
            KK.op("scalar", "activation", Yt, Yt, AF.Ln, bias=C1,
                  reads=[dRW, dSM], writes=[dRW])
            KK.op("scalar", "activation", X, ps, AF.Relu,
                  bias=l1col(4, o), reads=[dSM], writes=[dRW],
                  reads_bank=bank)
            KK.op("vector", "tensor_tensor",
                  STc(o)[:, tt * TW:(tt + 1) * TW], X, Yt, ALU.add,
                  reads=[dRW], writes=[dST])

        dense_fm(wdtT, D, B1c, dB1, evac_softplus)

        for wi, (wT, boff) in enumerate(((wBT, 0), (wCT, T))):
            ldw = K.dma("gpsimd", out=SMB[:, 0:NC * NST]
                        .rearrange("p (c g) -> p c g", c=NC),
                        in_=wT.rearrange("(c p) g -> p c g", p=128),
                        writes=[dSMB])
            for tt in range(NTT):
                bk = 6 + tt % 2
                chunks = [(SMB[:, k * NST:(k + 1) * NST],
                           B1c(k)[:, tt * TW:(tt + 1) * TW])
                          for k in range(NC)]
                mm_groups(K, [dict(bank=bk, ps=PS[bk][0:NST, 0:TW],
                                   chunks=chunks, reads=[dSMB, dB1],
                                   waits=[ldw])])
                K.op("scalar", "activation",
                     B2[0:NST, boff + tt * TW:boff + (tt + 1) * TW],
                     PS[bk][0:NST, 0:TW], AF.Identity,
                     bias=SM[0:NST, 336 + wi:337 + wi],
                     reads=[dSM], writes=[dB2], reads_bank=bk)


        def HCAR(ch):
            return SM[:, 370 + ch:371 + ch]

        def PACC(ch):
            return SM[:, 378 + ch:379 + ch]

        def yslice(ch, q):
            return F1[:, ch * SL + q * QW:ch * SL + (q + 1) * QW]

        dAp = [Dep(), Dep()]
        dBp = [Dep(), Dep()]
        dCp = [Dep(), Dep()]
        dDm = Dep()
        dCAR = Dep()
        dSUM = Dep()
        _bt2 = K.barrier_tokens()
        for _d in dAp + dBp + dCp + [dDm, dCAR, dSUM]:
            _d.r = list(_bt2)
        BQ2 = MSKB[:, 0:QW]
        CQ2 = MSKB[:, QW:2 * QW]

        def DMn(p):
            return B2[:, B2COLS - (2 + 2 * p) * QW:
                      B2COLS - (1 + 2 * p) * QW]

        def DMl(p):
            return B2[:, B2COLS - (1 + 2 * p) * QW:
                      B2COLS - 2 * p * QW]
        dDn = [Dep(), Dep()]
        dDl = [Dep(), Dep()]
        for _d in dDn + dDl:
            _d.r = list(_bt2)

        def slot(p, i):
            buf = RW if p == 0 else SCN
            return buf[:, i * QW:(i + 1) * QW]

        SDT8 = SM[:, 378:378 + NC]
        for b in range(B):
            K.op("vector", "memset", F1[:, 0:NC * SL], 0.0, writes=[dF1])
            for ch in range(NC):
                K.op("vector", "tensor_reduce", SM[:, 378 + ch:379 + ch],
                     STc(ch)[:, b * SL:(b + 1) * SL], AX.X, ALU.add,
                     reads=[dST], writes=[dSUM])
            PB = 404 + SUMW
            for n in range(NST):
                K.op("vector", "tensor_scalar",
                     SM[:, PB + n * NC:PB + (n + 1) * NC], SDT8,
                     acol(n), None, ALU.mult, reads=[dSUM, dSM],
                     writes=[dSUM])
            K.op("scalar", "activation", SM[:, PB:PB + SUMW],
                 SM[:, PB:PB + SUMW], AF.Exp, reads=[dSUM],
                 writes=[dSUM])
            for n in range(NST):
                for q in range(NQ):
                    csl = slice(b * SL + q * QW, b * SL + (q + 1) * QW)
                    K.mm(PS[2][:, 0:QW], SEL[0:NST, n * 128:(n + 1) * 128],
                         B2[0:NST, csl], 2, start=True, stop=True,
                         reads=[dB2, dSEL])
                    K.op("scalar", "activation", BQ2, PS[2][:, 0:QW],
                         AF.Identity, reads_bank=2, writes=[dMB])
                    K.mm(PS[3][:, 0:QW], SEL[0:NST, n * 128:(n + 1) * 128],
                         B2[0:NST, T + csl.start:T + csl.stop], 3,
                         start=True, stop=True, reads=[dB2, dSEL])
                    K.op("scalar", "activation", CQ2, PS[3][:, 0:QW],
                         AF.Identity, reads_bank=3, writes=[dMB])

                    def mstage1(ch, p):
                        Ap, Cp = slot(p, 0), slot(p, 2)
                        dts = STc(ch)[:, csl]
                        K.op("vector", "tensor_scalar", Ap, dts,
                             acol(n), -1e-38, ALU.mult, ALU.min,
                             reads=[dST, dSM], writes=[dAp[p]])
                        K.op("scalar", "activation", slot(p, 1), Ap,
                             AF.Exp, reads=[dAp[p]], writes=[dBp[p]])
                        K.op("scalar", "activation", Cp, Ap, AF.Ln,
                             scale=-1.0, reads=[dAp[p]],
                             writes=[dCp[p]])
                        K.op("scalar", "activation", Cp, Cp, AF.Exp,
                             scale=-1.0, reads=[dCp[p]], writes=[dCp[p]])

                    def mstage1b(ch, p):
                        Ap, Bp = slot(p, 0), slot(p, 1)
                        K.op("vector", "tensor_scalar", DMn(p), Bp,
                             -1.0, 1.0, ALU.mult, ALU.add,
                             reads=[dBp[p]], writes=[dDn[p]],
                             allow_lp=True)
                        K.op("vector", "tensor_scalar", DMl(p), Ap,
                             0.5, 1.0, ALU.mult, ALU.add,
                             reads=[dAp[p]], writes=[dDl[p]],
                             allow_lp=True)
                        K.op("vector", "tensor_scalar", Ap, Ap, -1e-4,
                             None, ALU.is_le, reads=[dAp[p]],
                             writes=[dAp[p]])

                    def mstage2(ch, p):
                        Ap, Bp, Cp = slot(p, 0), slot(p, 1), slot(p, 2)
                        xns = B1c(ch)[:, csl]
                        K.op("vector", "tensor_tensor", Cp, Cp, DMn(p),
                             ALU.mult, reads=[dDn[p], dCp[p]],
                             writes=[dCp[p]])
                        K.op("vector", "tensor_tensor", Cp, Cp, DMl(p),
                             ALU.subtract, reads=[dDl[p], dCp[p]],
                             writes=[dCp[p]])
                        K.op("vector", "tensor_tensor", Cp, Ap, Cp,
                             ALU.mult, reads=[dAp[p], dCp[p]],
                             writes=[dCp[p]])
                        K.op("vector", "tensor_tensor", Cp, Cp, DMl(p),
                             ALU.add, reads=[dDl[p], dCp[p]],
                             writes=[dCp[p]])
                        K.op("vector", "tensor_tensor", Cp, Cp, BQ2,
                             ALU.mult, reads=[dMB, dCp[p]],
                             writes=[dCp[p]])
                        K.op("vector", "tensor_tensor", Cp, Cp, xns,
                             ALU.mult, reads=[dB1, dCp[p]],
                             writes=[dCp[p]])
                        init = 0.0 if q == 0 else HCAR(ch)[:, 0:1]
                        rd = ([dBp[p], dCp[p]] if q == 0
                              else [dBp[p], dCp[p], dCAR])
                        K.op("vector", "tensor_tensor_scan", Ap, Bp,
                             Cp, init, ALU.mult, ALU.add, reads=rd,
                             writes=[dAp[p]])
                        if NQ > 1 and q < NQ - 1:
                            K.op("vector", "tensor_copy", HCAR(ch),
                                 Ap[:, QW - 1:QW], reads=[dAp[p]],
                                 writes=[dCAR])
                        if q == NQ - 1:
                            c_ = 404 + n * NC + ch
                            K.op("vector", "tensor_copy",
                                 SM[:, c_:c_ + 1], Ap[:, QW - 1:QW],
                                 reads=[dAp[p]], writes=[dSUM])
                        K.op("vector", "tensor_tensor", Cp, Ap, CQ2,
                             ALU.mult, reads=[dAp[p], dMB],
                             writes=[dCp[p]])
                        K.op("vector", "tensor_tensor", yslice(ch, q),
                             yslice(ch, q), Cp, ALU.add,
                             reads=[dCp[p], dF1], writes=[dF1])

                    prevc = None
                    for ch in range(NC):
                        mstage1(ch, ch % 2)
                        if prevc is not None:
                            mstage1b(prevc, (ch - 1) % 2)
                            mstage2(prevc, (ch - 1) % 2)
                        prevc = ch
                    mstage1b(prevc, (NC - 1) % 2)
                    mstage2(prevc, (NC - 1) % 2)
            FTMP = RW[:, 2 * QW:2 * QW + SL]
            dFT = [dCp[0]] if QW >= SL else [dCp[0], dAp[1]]
            for ch in range(NC):
                K.op("vector", "tensor_scalar", FTMP,
                     B1c(ch)[:, b * SL:(b + 1) * SL], l1col(2, ch), None,
                     ALU.mult, reads=[dB1, dSM], writes=dFT)
                K.op("vector", "tensor_tensor", F1[:, ch * SL:(ch + 1) * SL],
                     F1[:, ch * SL:(ch + 1) * SL], FTMP, ALU.add,
                     reads=dFT + [dF1], writes=[dF1])
                K.op("vector", "tensor_scalar", F1[:, ch * SL:(ch + 1) * SL],
                     F1[:, ch * SL:(ch + 1) * SL], l1col(3, ch), None,
                     ALU.mult, reads=[dSM, dF1], writes=[dF1])
            K.dma("sync", out=y_spill.rearrange("(c p) t -> p c t", p=128)
                  [:, :, b * SL:(b + 1) * SL],
                  in_=F1[:, 0:NC * SL].rearrange("p (c s) -> p c s", c=NC),
                  reads=[dF1], writes=[dYS])
            K.dma("sync", out=agm_in[b], in_=SM[:, 404:404 + 2 * SUMW],
                  reads=[dSUM], writes=[dAGM])

        agm_tok = K.coll("AllGather", ALU.bypass, [agm_in.opt()],
                         [agm_out.opt()], waits=K.barrier_tokens())
        dAGMo = Dep()
        dAGMo.did_write(agm_tok)

        STATE = SM[:, 404:404 + SUMW]
        Z0 = constcol(0.0)
        for b in range(B):
            tz = K.op("vector", "memset", STATE, 0.0,
                      writes=[dSM, dSUM])
            HPJ = F1[:, 0:2 * SUMW]
            for j in range(NCORES):
                ldj = K.dma("sync", out=HPJ, in_=agm_out[j, b],
                            reads=[dAGMo, dF1], writes=[dF1])
                K.op("vector", "tensor_scalar", HPJ[:, SUMW:2 * SUMW],
                     HPJ[:, SUMW:2 * SUMW], 1.0, SM[:, 396 + j:397 + j],
                     ALU.subtract, ALU.mult, reads=[dF1, dSM], writes=[dF1],
                     waits=[ldj])
                K.op("vector", "tensor_scalar", HPJ[:, SUMW:2 * SUMW],
                     HPJ[:, SUMW:2 * SUMW], 1.0, None, ALU.add,
                     reads=[dF1], writes=[dF1])
                K.op("vector", "tensor_scalar", HPJ[:, 0:SUMW],
                     HPJ[:, 0:SUMW], SM[:, 396 + j:397 + j], None,
                     ALU.mult, reads=[dF1, dSM], writes=[dF1])
                K.op("vector", "tensor_tensor", STATE, STATE,
                     HPJ[:, SUMW:2 * SUMW], ALU.mult, reads=[dF1, dSM],
                     writes=[dSM])
                K.op("vector", "tensor_tensor", STATE, STATE,
                     HPJ[:, 0:SUMW], ALU.add, reads=[dF1, dSM],
                     writes=[dSM])
            K.op("vector", "memset", F1[:, 0:NC * SL], 0.0, writes=[dF1])
            for n in range(NST):
                for q in range(NQ):
                    csl = slice(b * SL + q * QW, b * SL + (q + 1) * QW)
                    K.mm(PS[3][:, 0:QW], SEL[0:NST, n * 128:(n + 1) * 128],
                         B2[0:NST, T + csl.start:T + csl.stop], 3,
                         start=True, stop=True, reads=[dB2, dSEL])
                    K.op("scalar", "activation", CQ2, PS[3][:, 0:QW],
                         AF.Identity, reads_bank=3, writes=[dMB])

                    def cstage1(ch, p):
                        Ap = slot(p, 0)
                        K.op("vector", "tensor_scalar", Ap,
                             STc(ch)[:, csl], acol(n), -1e-38,
                             ALU.mult, ALU.min, reads=[dST, dSM],
                             writes=[dAp[p]])
                        K.op("scalar", "activation", slot(p, 1), Ap,
                             AF.Exp, reads=[dAp[p]], writes=[dBp[p]])

                    def cstage2(ch, p):
                        Ap, Bp, Cp = (slot(p, 0), slot(p, 1),
                                      slot(p, 2))
                        if q == 0:
                            init = STATE[:, n * NC + ch:
                                         n * NC + ch + 1]
                            rd = [dBp[p], dSM]
                        else:
                            init = HCAR(ch)[:, 0:1]
                            rd = [dBp[p], dCAR, dSM]
                        K.op("vector", "tensor_tensor_scan", Ap, Bp,
                             Z0.broadcast_to([128, QW]), init,
                             ALU.mult, ALU.add, reads=rd,
                             writes=[dAp[p]])
                        if NQ > 1 and q < NQ - 1:
                            K.op("vector", "tensor_copy", HCAR(ch),
                                 Ap[:, QW - 1:QW], reads=[dAp[p]],
                                 writes=[dCAR])
                        K.op("vector", "tensor_tensor", Cp, Ap, CQ2,
                             ALU.mult, reads=[dAp[p], dMB],
                             writes=[dCp[p]])
                        K.op("vector", "tensor_tensor", yslice(ch, q),
                             yslice(ch, q), Cp, ALU.add,
                             reads=[dCp[p], dF1], writes=[dF1])

                    prevc = None
                    for ch in range(NC):
                        cstage1(ch, ch % 2)
                        if prevc is not None:
                            cstage2(prevc, (ch - 1) % 2)
                        prevc = ch
                    cstage2(prevc, (NC - 1) % 2)
            for ch in range(NC):
                K.op("vector", "tensor_scalar", F1[:, ch * SL:(ch + 1) * SL],
                     F1[:, ch * SL:(ch + 1) * SL], l1col(3, ch), None,
                     ALU.mult, reads=[dSM, dF1], writes=[dF1])
            K.dma("gpsimd", out=y_spill.rearrange("(c p) t -> p c t", p=128)
                  [:, :, b * SL:(b + 1) * SL],
                  in_=F1[:, 0:NC * SL].rearrange("p (c s) -> p c s", c=NC),
                  reads=[dF1, dYS], writes=[dYS], accum_op=ALU.add)

        t1 = K.dma("sync", out=ST[:].rearrange("p (c t) -> p c t", c=NC),
                   in_=st_spill.rearrange("(c p) t -> p c t", p=128),
                   reads=[dSP], writes=[dST], waits=K.barrier_tokens())
        t2 = K.dma("gpsimd", out=ST[:].rearrange("p (c t) -> p c t", c=NC),
                   in_=y_spill.rearrange("(c p) t -> p c t", p=128),
                   reads=[dYS, dST], writes=[dST], accum_op=ALU.add,
                   waits=[t1])
        K.wait("gpsimd", [t2])

        phase_barrier()
        layernorm(STc, dST, B1c, dB1, gamma_col=lambda k: l1col(1, k),
                  rms=True, eps=EPS_RMS)

        HSW2 = min(1024, DFF)
        NHS2 = DFF // HSW2
        HCS2 = HSW2 // 128
        for hs in range(NHS2):
            wbuf, wdep = (WA, dWA) if hs % 2 == 0 else (WBU, dWB)
            ldw1 = K.dma("gpsimd",
                         out=wbuf[:, 0:NC * HSW2]
                         .rearrange("p (c h) -> p c h", c=NC),
                         in_=w1lT[:, hs * HSW2:(hs + 1) * HSW2]
                         .rearrange("(c p) h -> p c h", p=128),
                         writes=[wdep])
            pend = []
            for hc in range(HCS2):
                for tt in range(NTT):
                    bank = (hc * NTT + tt) % 4
                    chunks = [(wbuf[:, k * HSW2 + hc * 128:
                                    k * HSW2 + hc * 128 + 128],
                               B1c(k)[:, tt * TW:(tt + 1) * TW])
                              for k in range(NC)]

                    def ev_h2(KK, hc=hc, tt=tt, bank=bank, hs=hs):
                        KK.op("scalar", "activation",
                              B2[:, hc * T + tt * TW:hc * T + (tt + 1) * TW],
                              PS[bank][:, 0:TW], AF.Gelu_apprx_tanh,
                              bias=b1lcol(hs * HCS2 + hc),
                              reads=[dSM], writes=[dB2], reads_bank=bank)
                    pend.append(dict(bank=bank, ps=PS[bank][:, 0:TW],
                                     chunks=chunks, reads=[wdep, dB1],
                                     waits=[ldw1], evac=ev_h2))
                    if len(pend) == 4:
                        mm_groups(K, pend)
                        pend = []
            if pend:
                mm_groups(K, pend)
            wb2, wd2 = (WBU, dWB) if hs % 2 == 0 else (WA, dWA)
            ldw2 = K.dma("gpsimd",
                         out=wb2[:, 0:HCS2 * D]
                         .rearrange("p (c o) -> p c o", c=HCS2),
                         in_=w2lT[hs * HSW2:(hs + 1) * HSW2, :]
                         .rearrange("(c p) o -> p c o", p=128),
                         writes=[wd2])
            pend = []
            for tt in range(NTT):
                for oc in range(NC):
                    bank = 4 + (oc + tt) % 2
                    chunks = [(wb2[:, c * D + oc * 128:
                                   c * D + oc * 128 + 128],
                               B2[:, c * T + tt * TW:c * T + (tt + 1) * TW])
                              for c in range(HCS2)]

                    def ev_o2(KK, oc=oc, tt=tt, bank=bank, hs=hs):
                        if hs == 0:
                            KK.op("vector", "scalar_tensor_tensor",
                                  STc(oc)[:, tt * TW:(tt + 1) * TW],
                                  PS[bank][:, 0:TW], l1col(5, oc),
                                  STc(oc)[:, tt * TW:(tt + 1) * TW],
                                  ALU.add, ALU.add,
                                  reads=[dSM, dST], writes=[dST],
                                  reads_bank=bank)
                        else:
                            KK.op("vector", "tensor_tensor",
                                  STc(oc)[:, tt * TW:(tt + 1) * TW],
                                  STc(oc)[:, tt * TW:(tt + 1) * TW],
                                  PS[bank][:, 0:TW], ALU.add,
                                  reads=[dST], writes=[dST],
                                  reads_bank=bank)
                    pend.append(dict(bank=bank, ps=PS[bank][:, 0:TW],
                                     chunks=chunks, reads=[wd2, dB2],
                                     waits=[ldw2], evac=ev_o2))
                    if len(pend) == 2:
                        mm_groups(K, pend)
                        pend = []
            if pend:
                mm_groups(K, pend)

    if upto >= 6:
        wgq = din("wgq", [GI, D])
        wuq = din("wuq", [GI, D])
        wdq = din("wdq", [D, GI])
        dqgT = dint("dqgT", [D * GI], BF16).rearrange("(a b) -> a b", a=D)
        dquT = dint("dquT", [D * GI], BF16).rearrange("(a b) -> a b", a=D)
        dqdT = dint("dqdT", [GI * D], BF16).rearrange("(a b) -> a b", a=GI)
        g_raw = dint("g_raw", [GI * T], BF16).rearrange("(a b) -> a b", a=GI)
        u_raw = dint("u_raw", [GI * T], BF16).rearrange("(a b) -> a b", a=GI)
        g2_raw = dint("g2_raw", [GI * T], BF16).rearrange("(a b) -> a b",
                                                          a=GI)
        uq_st = dint("uq_st", [GI * T], BF16).rearrange("(a b) -> a b", a=GI)
        oq_st = dint("oq_st", [GI * T], BF16).rearrange("(a b) -> a b", a=GI)
        amax_in = dint("amax_in", [16])
        amax_out = dint("amax_out", [NCORES * 16])
        dDQ = Dep()
        dGRg = Dep()
        dGRu = Dep()
        dGRg2 = Dep()
        dGRuq = Dep()
        dGRoq = Dep()
        dAMX = Dep()
        NGC = GI // 128

        phase_barrier()
        layernorm(STc, dST, B1c, dB1)

        dITMs = {}
        dAM = Dep()

        def round_int(ap_f32, n, dep=None, eng="vector", isl=None,
                      ibuf=None):
            dep = dep or dF1
            base, cw = isl if isl is not None else (0, 512)
            ib = ibuf if ibuf is not None else ITMP
            key = (id(ib), base, cw)
            if key not in dITMs:
                dITMs[key] = Dep()
                dITMs[key].r = K.barrier_tokens()
            dI = dITMs[key]
            for s0 in range(0, n, cw):
                w = min(cw, n - s0)
                K.op(eng, "tensor_copy", ib[:, base:base + w],
                     ap_f32[:, s0:s0 + w], reads=[dep], writes=[dI])
                K.op(eng, "tensor_copy", ap_f32[:, s0:s0 + w],
                     ib[:, base:base + w], reads=[dI], writes=[dep])

        def dequant_transpose(w_dram, dqT_dram, O, I):
            NB = I // 32
            for r in range(O // 128):
                ldw = K.dma("sync", out=F1[:, 0:I],
                            in_=w_dram[r * 128:(r + 1) * 128, :],
                            writes=[dF1])
                K.op("vector", "tensor_reduce", RW[:, 0:NB],
                     F1[:, 0:I].rearrange("p (nb w) -> p nb w", w=32),
                     AX.X, ALU.max, apply_absolute_value=True,
                     reads=[dF1], writes=[dRW], waits=[ldw])
                K.op("vector", "tensor_scalar", RW[:, NB:2 * NB],
                     RW[:, 0:NB], 1.0 / 7.0, 1e-12, ALU.mult, ALU.max,
                     reads=[dRW], writes=[dRW])
                K.op("vector", "reciprocal", RW[:, 2 * NB:3 * NB],
                     RW[:, NB:2 * NB], reads=[dRW], writes=[dRW])
                K.op("vector", "tensor_tensor", F1[:, 0:I]
                     .rearrange("p (nb w) -> p nb w", w=32),
                     F1[:, 0:I].rearrange("p (nb w) -> p nb w", w=32),
                     RW[:, 2 * NB:3 * NB].unsqueeze(2)
                     .broadcast_to([128, NB, 32]), ALU.mult,
                     reads=[dRW], writes=[dF1])
                round_int(F1[:, 0:I], I)
                K.op("vector", "tensor_scalar", F1[:, 0:I], F1[:, 0:I],
                     7.0, -8.0, ALU.min, ALU.max, reads=[dF1], writes=[dF1])
                K.op("vector", "tensor_tensor", F1[:, 0:I]
                     .rearrange("p (nb w) -> p nb w", w=32),
                     F1[:, 0:I].rearrange("p (nb w) -> p nb w", w=32),
                     RW[:, NB:2 * NB].unsqueeze(2)
                     .broadcast_to([128, NB, 32]), ALU.mult,
                     reads=[dRW], writes=[dF1])
                for ic in range(I // 128):
                    tb = 2 + ic % 2
                    K.transpose(PS[tb][:, 0:128],
                                F1[:, ic * 128:(ic + 1) * 128], IDENT, tb,
                                reads=[dF1, dSM])
                    K.op("vector", "tensor_copy",
                         B2[:, ic * 128:(ic + 1) * 128], PS[tb][:, 0:128],
                         reads_bank=tb, writes=[dB2])
                K.dma("sync",
                      out=dqT_dram[:, r * 128:(r + 1) * 128]
                      .rearrange("(ic p) o -> p ic o", p=128),
                      in_=B2[:, 0:I].rearrange("p (ic o) -> p ic o", o=128),
                      reads=[dB2], writes=[dDQ])

        dequant_transpose(wgq, dqgT, GI, D)
        dequant_transpose(wuq, dquT, GI, D)
        dequant_transpose(wdq, dqdT, D, GI)

        AMG = SM[:, 386:387]
        AMU = SM[:, 387:388]
        for k in range(NC):
            K.op("vector", "tensor_reduce", AMU, B1c(k), AX.X, ALU.max,
                 apply_absolute_value=True, reads=[dB1], writes=[dAM])
            if k == 0:
                K.op("vector", "tensor_copy", AMG, AMU, reads=[dAM],
                     writes=[dAM])
            else:
                K.op("vector", "tensor_tensor", AMG, AMG, AMU, ALU.max,
                     reads=[dAM], writes=[dAM])

        def part_reduce_max(col_ap, dst_row_ap, bank):
            K.transpose(PS[bank][0:1, 0:128], col_ap, IDENT, bank,
                        reads=[dAM, dSM])
            K.op("vector", "tensor_reduce", dst_row_ap, PS[bank][0:1, 0:128],
                 AX.X, ALU.max, reads_bank=bank, writes=[dAM])

        def qdq_cols(scale_row, rs_col, s_col, bank):
            K.op("vector", "tensor_scalar", scale_row, scale_row,
                 1.0 / 32767.0, 1e-12, ALU.mult, ALU.max, reads=[dAM],
                 writes=[dAM])
            K.mm(PS[bank][:, 0:1], ONES[0:1, 0:128], scale_row, bank,
                 start=True, stop=True, reads=[dAM, dSM])
            K.op("vector", "tensor_copy", s_col, PS[bank][:, 0:1],
                 reads_bank=bank, writes=[dSM])
            K.op("vector", "reciprocal", rs_col, s_col, reads=[dSM],
                 writes=[dSM])

        XMROW = SM[0:1, 440:456]
        K.op("vector", "memset", XMROW, 0.0, writes=[dAM])

        def set_xmrow():
            K.op("vector", "tensor_copy", SM[0:1, 440:442],
                 SM[0:1, 386:388], reads=[dAM], writes=[dAM])
        part_reduce_max(AMG, SM[0:1, 386:387], 2)
        set_xmrow()
        t_am = K.dma("sync", out=amax_in, in_=XMROW, reads=[dAM],
                     writes=[dAMX])
        ag1 = K.coll("AllGather", ALU.bypass, [amax_in.opt()],
                     [amax_out.opt()], waits=[t_am] + K.barrier_tokens())
        dAMXo = Dep()
        dAMXo.did_write(ag1)
        t_l = K.dma("sync", out=SM[0:1, 400:400 + NCORES],
                    in_=amax_out.rearrange("(r c) -> r c", c=16)[:, 0]
                    .rearrange("(j r) -> j r", j=1),
                    reads=[dAMXo], writes=[dAM],
                    allow_slow_non_contiguous=True)
        K.op("vector", "tensor_reduce", SM[0:1, 386:387],
             SM[0:1, 400:400 + NCORES],
             AX.X, ALU.max, reads=[dAM], writes=[dAM], waits=[t_l])
        RS_X = SM[:, 408:409]
        S_X = SM[:, 409:410]
        qdq_cols(SM[0:1, 386:387], RS_X, S_X, 2)
        for k in range(NC):
            for tt in range(NTT):
                sl_ = slice(tt * TW, (tt + 1) * TW)
                K.op("vector", "tensor_scalar", F1[:, 0:TW], B1c(k)[:, sl_],
                     RS_X, None, ALU.mult, reads=[dB1, dSM], writes=[dF1])
                round_int(F1[:, 0:TW], TW)
                K.op("vector", "tensor_scalar", F1[:, 0:TW], F1[:, 0:TW],
                     32767.0, -32768.0, ALU.min, ALU.max, reads=[dF1],
                     writes=[dF1])
                K.op("vector", "tensor_scalar",
                     B2c(k)[:, sl_] if False else
                     B2[:, k * T + sl_.start:k * T + sl_.stop],
                     F1[:, 0:TW], S_X, None, ALU.mult,
                     reads=[dF1, dSM], writes=[dB2])

        AMG2 = SM[:, 410:411]
        first_flag = [True]

        def raw_mm(dqT, out_dram, amax_col, outdep):
            first = [True]
            HWO = min(512, GI)
            for half in range(GI // HWO):
                wb_, wd_ = (WA, dWA) if half % 2 == 0 else (WBU, dWB)
                ldw = K.dma("sync", out=wb_[:, 0:NC * HWO]
                            .rearrange("p (c o) -> p c o", c=NC),
                            in_=dqT[:, half * HWO:(half + 1) * HWO]
                            .rearrange("(c p) o -> p c o", p=128),
                            reads=[dDQ], writes=[wd_])
                pend = []
                for oc in range(HWO // 128):
                    og = half * (HWO // 128) + oc
                    for tt in range(NTT):
                        bank = (oc * NTT + tt) % 4
                        chunks = [(wb_[:, k * HWO + oc * 128:
                                       k * HWO + oc * 128 + 128],
                                   B2[:, k * T + tt * TW:
                                      k * T + (tt + 1) * TW])
                                  for k in range(NC)]

                        def ev(KK, og=og, tt=tt, bank=bank, first=first):
                            KK.op("vector", "tensor_copy", MSKB[:, 0:TW],
                                  PS[bank][:, 0:TW], reads_bank=bank,
                                  writes=[dMB])
                            KK.dma("gpsimd",
                                   out=out_dram[og * 128:(og + 1) * 128,
                                                tt * TW:(tt + 1) * TW],
                                   in_=MSKB[:, 0:TW], reads=[dMB],
                                   writes=[outdep])
                            KK.op("vector", "tensor_reduce", AMU,
                                  PS[bank][:, 0:TW], AX.X, ALU.max,
                                  apply_absolute_value=True,
                                  reads_bank=bank, writes=[dAM])
                            if first[0]:
                                KK.op("vector", "tensor_copy", amax_col,
                                      AMU, reads=[dAM], writes=[dAM])
                                first[0] = False
                            else:
                                KK.op("vector", "tensor_tensor", amax_col,
                                      amax_col, AMU, ALU.max, reads=[dAM],
                                      writes=[dAM])
                        pend.append(dict(bank=bank, ps=PS[bank][:, 0:TW],
                                         chunks=chunks,
                                         reads=[wd_, dB2], waits=[ldw],
                                         evac=ev))
                        if len(pend) == 4:
                            mm_groups(K, pend)
                            pend = []
                if pend:
                    mm_groups(K, pend)

        raw_mm(dqgT, g_raw, AMG, dGRg)
        raw_mm(dquT, u_raw, AMG2, dGRu)
        part_reduce_max(AMG, SM[0:1, 386:387], 2)
        part_reduce_max(AMG2, SM[0:1, 387:388], 3)
        set_xmrow()
        t_am = K.dma("sync", out=amax_in, in_=XMROW, reads=[dAM],
                     writes=[dAMX], waits=dAMXo.for_write())
        ag2 = K.coll("AllGather", ALU.bypass, [amax_in.opt()],
                     [amax_out.opt()], waits=[t_am] + K.barrier_tokens())
        dAMXo.did_write(ag2)
        t_l = K.dma("sync", out=SM[0:1, 400:400 + NCORES],
                    in_=amax_out.rearrange("(r c) -> r c", c=16)[:, 0]
                    .rearrange("(j r) -> j r", j=1),
                    reads=[dAMXo], writes=[dAM],
                    allow_slow_non_contiguous=True)
        t_l2 = K.dma("sync", out=SM[0:1, 424:424 + NCORES],
                     in_=amax_out.rearrange("(r c) -> r c", c=16)[:, 1]
                     .rearrange("(j r) -> j r", j=1),
                     reads=[dAMXo], writes=[dAM],
                     allow_slow_non_contiguous=True)
        K.op("vector", "tensor_reduce", SM[0:1, 386:387],
             SM[0:1, 400:400 + NCORES],
             AX.X, ALU.max, reads=[dAM], writes=[dAM], waits=[t_l])
        K.op("vector", "tensor_reduce", SM[0:1, 387:388],
             SM[0:1, 424:424 + NCORES],
             AX.X, ALU.max, reads=[dAM], writes=[dAM], waits=[t_l2])
        RS_G = SM[:, 411:412]
        S_G = SM[:, 412:413]
        RS_U = SM[:, 413:414]
        S_U = SM[:, 414:415]
        qdq_cols(SM[0:1, 386:387], RS_G, S_G, 2)
        qdq_cols(SM[1:2, 386:387] if False else SM[0:1, 387:388],
                 RS_U, S_U, 3)

        def qdq_tile(ap, rs_col, s_col, n, dep=None, eng="vector",
                     isl=None, src=None, srcreads=(), ibuf=None):
            dep = dep or dF1
            if src is None:
                K.op(eng, "tensor_scalar", ap, ap, rs_col, None, ALU.mult,
                     reads=[dSM, dep], writes=[dep])
            else:
                K.op(eng, "tensor_scalar", ap, src, rs_col, None, ALU.mult,
                     reads=[dSM] + list(srcreads), writes=[dep])
            round_int(ap, n, dep, eng, isl, ibuf)
            K.op(eng, "tensor_scalar", ap, ap, 32767.0, -32768.0,
                 ALU.min, ALU.max, reads=[dep], writes=[dep])
            K.op(eng, "tensor_scalar", ap, ap, s_col, None, ALU.mult,
                 reads=[dSM, dep], writes=[dep])

        AM3 = SM[:, 415:416]
        firstA = [True]
        NPAR = 4
        dFp = [Dep() for _ in range(NPAR)]
        dB1p = [Dep() for _ in range(NPAR)]
        _bt4 = K.barrier_tokens()
        for _d in dFp + dB1p:
            _d.r = list(_bt4)

        def ptile(up_):
            FA = F1[:, up_ * 2 * TW:up_ * 2 * TW + TW]
            FB = F1[:, up_ * 2 * TW + TW:up_ * 2 * TW + 2 * TW]
            GL = B1[:, (up_ * 3) * TW:(up_ * 3 + 1) * TW]
            UL = B1[:, (up_ * 3 + 1) * TW:(up_ * 3 + 2) * TW]
            UQ = B1[:, (up_ * 3 + 2) * TW:(up_ * 3 + 3) * TW]
            ib = ITMP if up_ < 2 else ITMP2
            return (FA, FB, GL, UL, UQ, dFp[up_], dB1p[up_],
                    ((up_ % 2) * 256, 256), ib)

        for og in range(NGC):
            for tt in range(NTT):
                up_ = (og * NTT + tt) % NPAR
                FA, FB, GL, UL, UQ, dFx, dBx, isl_, ib_ = ptile(up_)
                r0 = slice(og * 128, (og + 1) * 128)
                c0 = slice(tt * TW, (tt + 1) * TW)
                ldg = K.dma("sync", out=GL, in_=g_raw[r0, c0],
                            reads=[dGRg], writes=[dBx])
                ldu = K.dma("scalar", out=UL, in_=u_raw[r0, c0],
                            reads=[dGRu], writes=[dBx])
                qdq_tile(FA, RS_G, S_G, TW, dFx, isl=isl_,
                         src=GL, srcreads=[dBx, dFx], ibuf=ib_)
                K.op("scalar", "activation", FB, FA, AF.Sigmoid,
                     reads=[dFx], writes=[dFx])
                K.op("vector", "tensor_scalar", FB, FB, 65536.0, None,
                     ALU.mult, reads=[dFx], writes=[dFx])
                round_int(FB, TW, dFx, isl=isl_, ibuf=ib_)
                K.op("vector", "tensor_scalar", FB, FB, 65535.0, 0.0,
                     ALU.min, ALU.max, reads=[dFx], writes=[dFx])
                K.op("vector", "tensor_scalar", FB, FB, 1.0 / 65536.0,
                     None, ALU.mult, reads=[dFx], writes=[dFx])
                K.op("vector", "tensor_tensor", FA, FA, FB, ALU.mult,
                     reads=[dFx], writes=[dFx])
                K.dma("gpsimd", out=g2_raw[r0, c0], in_=FA,
                      reads=[dFx], writes=[dGRg2])
                K.op("vector", "tensor_reduce", AMU, FA, AX.X, ALU.max,
                     apply_absolute_value=True, reads=[dFx], writes=[dAM])
                if firstA[0]:
                    K.op("vector", "tensor_copy", AM3, AMU, reads=[dAM],
                         writes=[dAM])
                    firstA[0] = False
                else:
                    K.op("vector", "tensor_tensor", AM3, AM3, AMU,
                         ALU.max, reads=[dAM], writes=[dAM])
                K.op("vector", "tensor_scalar", FB, UL, RS_U, None,
                     ALU.mult, reads=[dSM, dBx, dFx], writes=[dFx])
                round_int(FB, TW, dFx, eng="vector", isl=isl_, ibuf=ib_)
                K.op("vector", "tensor_scalar", FB, FB, 32767.0,
                     -32768.0, ALU.min, ALU.max, reads=[dFx],
                     writes=[dFx])
                K.op("vector", "tensor_scalar", UQ, FB, S_U, None,
                     ALU.mult, reads=[dSM, dFx], writes=[dBx],
                     allow_lp=True)
                K.dma("sync", out=uq_st[r0, c0], in_=UQ, reads=[dBx],
                      writes=[dGRuq])

        part_reduce_max(AM3, SM[0:1, 386:387], 2)
        set_xmrow()
        t_am = K.dma("sync", out=amax_in, in_=XMROW, reads=[dAM],
                     writes=[dAMX], waits=dAMXo.for_write())
        ag3 = K.coll("AllGather", ALU.bypass, [amax_in.opt()],
                     [amax_out.opt()], waits=[t_am] + K.barrier_tokens())
        dAMXo.did_write(ag3)
        t_l = K.dma("sync", out=SM[0:1, 400:400 + NCORES],
                    in_=amax_out.rearrange("(r c) -> r c", c=16)[:, 0]
                    .rearrange("(j r) -> j r", j=1),
                    reads=[dAMXo], writes=[dAM],
                    allow_slow_non_contiguous=True)
        K.op("vector", "tensor_reduce", SM[0:1, 386:387],
             SM[0:1, 400:400 + NCORES],
             AX.X, ALU.max, reads=[dAM], writes=[dAM], waits=[t_l])
        RS_2 = SM[:, 416:417]
        S_2 = SM[:, 417:418]
        qdq_cols(SM[0:1, 386:387], RS_2, S_2, 2)

        AM4 = SM[:, 418:419]
        firstB = [True]
        for og in range(NGC):
            for tt in range(NTT):
                up_ = (og * NTT + tt) % NPAR
                FA, FB, GL, UL, UQ, dFx, dBx, isl_, ib_ = ptile(up_)
                r0 = slice(og * 128, (og + 1) * 128)
                c0 = slice(tt * TW, (tt + 1) * TW)
                K.dma("sync", out=GL, in_=g2_raw[r0, c0], reads=[dGRg2],
                      writes=[dBx])
                K.dma("scalar", out=UL, in_=uq_st[r0, c0], reads=[dGRuq],
                      writes=[dBx])
                qdq_tile(FA, RS_2, S_2, TW, dFx, isl=isl_,
                         src=GL, srcreads=[dBx, dFx], ibuf=ib_)
                K.op("vector", "tensor_tensor", FA, FA, UL, ALU.mult,
                     reads=[dBx, dFx], writes=[dFx])
                K.dma("gpsimd", out=oq_st[r0, c0], in_=FA, reads=[dFx],
                      writes=[dGRoq])
                K.op("vector", "tensor_reduce", AMU, FA, AX.X, ALU.max,
                     apply_absolute_value=True, reads=[dFx], writes=[dAM])
                if firstB[0]:
                    K.op("vector", "tensor_copy", AM4, AMU, reads=[dAM],
                         writes=[dAM])
                    firstB[0] = False
                else:
                    K.op("vector", "tensor_tensor", AM4, AM4, AMU,
                         ALU.max, reads=[dAM], writes=[dAM])

        part_reduce_max(AM4, SM[0:1, 386:387], 2)
        set_xmrow()
        t_am = K.dma("sync", out=amax_in, in_=XMROW, reads=[dAM],
                     writes=[dAMX], waits=dAMXo.for_write())
        ag4 = K.coll("AllGather", ALU.bypass, [amax_in.opt()],
                     [amax_out.opt()], waits=[t_am] + K.barrier_tokens())
        dAMXo.did_write(ag4)
        t_l = K.dma("sync", out=SM[0:1, 400:400 + NCORES],
                    in_=amax_out.rearrange("(r c) -> r c", c=16)[:, 0]
                    .rearrange("(j r) -> j r", j=1),
                    reads=[dAMXo], writes=[dAM],
                    allow_slow_non_contiguous=True)
        K.op("vector", "tensor_reduce", SM[0:1, 386:387],
             SM[0:1, 400:400 + NCORES],
             AX.X, ALU.max, reads=[dAM], writes=[dAM], waits=[t_l])
        RS_O = SM[:, 419:420]
        S_O = SM[:, 420:421]
        qdq_cols(SM[0:1, 386:387], RS_O, S_O, 2)

        for tt in range(NTT):
            hcn = NGC // 2
            ldo = K.dma("sync", out=B1[:, 0:hcn * TW]
                        .rearrange("p (c w) -> p c w", c=hcn),
                        in_=oq_st.rearrange("(c p) t -> p c t", p=128)
                        [:, 0:hcn, tt * TW:(tt + 1) * TW],
                        reads=[dGRoq], writes=[dB1])
            ldo2 = K.dma("gpsimd", out=B1[:, hcn * TW:NGC * TW]
                         .rearrange("p (c w) -> p c w", c=NGC - hcn),
                         in_=oq_st.rearrange("(c p) t -> p c t", p=128)
                         [:, hcn:NGC, tt * TW:(tt + 1) * TW],
                         reads=[dGRoq], writes=[dB1])
            for c_ in range(NGC):
                FQ = F1[:, (c_ % 8) * TW:(c_ % 8 + 1) * TW]
                dFx = dFp[(c_ % 8) // 2]
                wt = [ldo if c_ < hcn else ldo2]
                K.op("vector", "tensor_scalar", FQ, B1[:, c_ * TW:
                     (c_ + 1) * TW], RS_O, None, ALU.mult,
                     reads=[dSM, dB1, dFx], writes=[dFx], waits=wt)
                round_int(FQ, TW, dFx, isl=((c_ % 2) * 256, 256))
                K.op("vector", "tensor_scalar", FQ, FQ, 32767.0,
                     -32768.0, ALU.min, ALU.max, reads=[dFx],
                     writes=[dFx])
                K.op("vector", "tensor_scalar", B1[:, c_ * TW:
                     (c_ + 1) * TW], FQ, S_O, None, ALU.mult,
                     reads=[dSM, dFx], writes=[dB1], allow_lp=True)
            for grp in range(D // 256):
                wb_, wd_ = (WA, dWA) if grp % 2 == 0 else (WBU, dWB)
                wwt = K.barrier_tokens() if tt == 0 and grp < 2 else ()
                ldw = K.dma("sync", out=wb_[:, 0:NGC * 256]
                            .rearrange("p (c o) -> p c o", c=NGC),
                            in_=dqdT[:, grp * 256:(grp + 1) * 256]
                            .rearrange("(c p) o -> p c o", p=128),
                            reads=[dDQ], writes=[wd_], waits=wwt)
                pend = []
                for oc2 in range(2):
                    oc = grp * 2 + oc2
                    bank = oc2 % 2
                    chunks = [(wb_[:, c * 256 + oc2 * 128:
                                   c * 256 + oc2 * 128 + 128],
                               B1[:, c * TW:(c + 1) * TW])
                              for c in range(NGC)]

                    def ev_d(KK, oc=oc, tt=tt, bank=bank):
                        KK.op("vector", "tensor_tensor",
                              STc(oc)[:, tt * TW:(tt + 1) * TW],
                              STc(oc)[:, tt * TW:(tt + 1) * TW],
                              PS[bank][:, 0:TW], ALU.add,
                              reads=[dST], writes=[dST], reads_bank=bank)
                    pend.append(dict(bank=bank, ps=PS[bank][:, 0:TW],
                                     chunks=chunks, reads=[wd_, dB1],
                                     waits=[ldw, ldo], evac=ev_d))
                mm_groups(K, pend)

    if dbg and upto >= 4:
        dbg_msks = nc.dram_tensor("dbg_msks", [9, T], BF16,
                                  kind="ExternalOutput").ap()
        dbg_xrt = nc.dram_tensor("dbg_xrt", [NST, T], F32,
                                 kind="ExternalOutput").ap()
        dbg_xn = nc.dram_tensor("dbg_xn", [D, T], BF16,
                                kind="ExternalOutput").ap()
        K.dma("sync", out=dbg_msks, in_=MSKS[:, :],
              waits=K.barrier_tokens())
        K.dma("sync", out=dbg_xrt, in_=XRT[:, :], waits=K.barrier_tokens())
        K.dma("sync", out=dbg_xn.rearrange("(c p) t -> p c t", p=128),
              in_=B1[:, 0:NC * T].rearrange("p (c t) -> p c t", c=NC),
              waits=K.barrier_tokens())

    st_tok = K.dma("sync", out=xout.rearrange("(c p) t -> p c t", p=128),
                   in_=ST[:].rearrange("p (c t) -> p c t", c=NC),
                   reads=[dST], waits=K.barrier_tokens())
    K.wait("sync", [st_tok])

    with ExitStack() as ctx:
        K.build_block(ctx)
    return nc



def _shard_inputs(inputs):
    f32 = np.float32
    S = SL * NCORES
    T = B * SL
    x = np.asarray(inputs["L0_attn_x"], f32)
    wqT = np.ascontiguousarray(np.asarray(inputs["L0_attn_Wq"], f32).T)
    wkT = np.ascontiguousarray(np.asarray(inputs["L0_attn_Wk"], f32).T)
    wvT = np.ascontiguousarray(np.asarray(inputs["L0_attn_Wv"], f32).T)
    woT = np.ascontiguousarray(np.asarray(inputs["L0_attn_Wo"], f32).T)
    bqk = np.ascontiguousarray(np.stack([
        np.asarray(inputs["L0_attn_bq"], f32),
        np.asarray(inputs["L0_attn_bk"], f32),
        np.asarray(inputs["L0_attn_bv"], f32),
        np.asarray(inputs["L0_attn_bo"], f32)]))
    com = dict(wqT=wqT, wkT=wkT, wvT=wvT, woT=woT, bqk=bqk)
    if "L1_attn_Wdt" in inputs:
        g = lambda k: np.asarray(inputs[k], f32)
        com.update(
            wdtT=np.ascontiguousarray(g("L1_attn_Wdt").T),
            wBT=np.ascontiguousarray(g("L1_attn_WB").T),
            wCT=np.ascontiguousarray(g("L1_attn_WC").T),
            w1lT=np.ascontiguousarray(g("L1_attn_W1").T),
            w2lT=np.ascontiguousarray(g("L1_attn_W2").T),
            l1vec=np.ascontiguousarray(np.stack([
                g("L1_attn_norm1_w"), g("L1_attn_norm2_w"),
                g("L1_attn_D_param"), g("L1_attn_scale"),
                g("L1_attn_bdt"), g("L1_attn_b2")])),
            l1b1=np.ascontiguousarray(g("L1_attn_b1")),
            l1bBC=np.ascontiguousarray(np.stack([g("L1_attn_bB"),
                                                 g("L1_attn_bC")])),
            l1A=np.ascontiguousarray(g("L1_attn_A_log")),
        )
    if "L1_ffn_w_gate" in inputs:
        com.update(
            wgq=np.ascontiguousarray(np.asarray(inputs["L1_ffn_w_gate"],
                                                f32)),
            wuq=np.ascontiguousarray(np.asarray(inputs["L1_ffn_w_up"], f32)),
            wdq=np.ascontiguousarray(np.asarray(inputs["L1_ffn_w_down"],
                                                f32)),
        )
    if "L0_ffn_w1" in inputs:
        com.update(
            moe_w1=np.ascontiguousarray(np.asarray(inputs["L0_ffn_w1"], f32)),
            moe_w2=np.ascontiguousarray(np.asarray(inputs["L0_ffn_w2"], f32)),
            moe_b1=np.ascontiguousarray(np.asarray(inputs["L0_ffn_b1"], f32)),
            moe_b2=np.ascontiguousarray(np.asarray(inputs["L0_ffn_b2"], f32)),
            wgrT=np.ascontiguousarray(
                np.asarray(inputs["L0_ffn_wg_red"], f32).T),
            wg=np.ascontiguousarray(np.asarray(inputs["L0_ffn_wg"], f32)),
        )
    maps = []
    for i in range(NCORES):
        sl = x[:, i * SL:(i + 1) * SL, :].reshape(T, D)
        m = dict(xin=np.ascontiguousarray(sl.T), **com)
        if "L1_attn_Wdt" in inputs:
            m["cmask"] = (np.arange(8) < i).astype(f32)
        maps.append(m)
    return maps


def kernel(**inputs):
    S = SL * NCORES
    nc = build_program()
    in_maps = _shard_inputs(inputs)
    res = run_bass_kernel_spmd(nc, in_maps, core_ids=list(range(NCORES)))
    out = np.empty((B, S, D), np.float32)
    for i in range(NCORES):
        sh = np.asarray(res.results[i]["xout"])
        out[:, i * SL:(i + 1) * SL, :] = sh.T.reshape(B, SL, D)
    return out


# version 25 (final)
# speedup vs baseline: 1.0106x; 1.0106x over previous
import numpy as np
from contextlib import ExitStack

import concourse.bass as bass
import concourse.mybir as mybir
from concourse.bass_utils import run_bass_kernel_spmd

F32 = mybir.dt.float32
BF16 = mybir.dt.bfloat16
I32 = mybir.dt.int32
AF = mybir.ActivationFunctionType
ALU = mybir.AluOpType
AX = mybir.AxisListType

NCORES = 8
B = 4
SL = 512
D = 1024
NH = 16
MOE_H = 4096
DFF = 4096
GI = 3072
NST = 16
EPS_LN = 1e-5
EPS_RMS = 1e-6
EPS_GATE = 1e-4
MOE_E = 8
DK = 64
UPTO = 99

ENGINES = ("sync", "gpsimd", "tensor", "vector", "scalar")



class Dep:
    __slots__ = ("w", "r")

    def __init__(self):
        self.w = None
        self.r = []

    def for_read(self):
        return [self.w]

    def for_write(self):
        return [self.w] + self.r

    def did_read(self, tok):
        if tok is not None:
            self.r.append(tok)

    def did_write(self, tok):
        self.w = tok
        self.r = []


class KB:
    SEMS = ("tS", "vS", "aS", "gS", "qS0", "qS1", "qG0", "qG1",
            "qA0", "qA1", "cc")

    def __init__(self, nc):
        self.nc = nc
        self.ops = {e: [] for e in ENGINES}
        self.cnt = {s: 0 for s in self.SEMS}
        self.bank_last_mm = {}
        self.bank_readers = {}
        self.sems = {}
        self.n = 0
        self._qn = {}

    def emit(self, eng, build, waits=(), inc=None):
        tok = None
        incs = []
        if inc is not None:
            name, n = inc if isinstance(inc, tuple) else (inc, 1)
            self.cnt[name] += n
            tok = (name, self.cnt[name])
            incs.append((name, n))
        ws = tuple(w for w in waits if w is not None)
        self.ops[eng].append((ws, build, incs))
        self.n += 1
        return tok

    @staticmethod
    def _collect(reads, writes, extra):
        w = list(extra)
        for d in reads:
            w += d.for_read()
        for d in writes:
            w += d.for_write()
        return w

    @staticmethod
    def _record(tok, reads, writes):
        for d in reads:
            d.did_read(tok)
        for d in writes:
            d.did_write(tok)

    def mm(self, ps_ap, lhsT, rhs, bank, start, stop, reads=(), waits=()):
        w = self._collect(reads, (), waits)
        w.append(self.bank_last_mm.get(bank))
        if start and bank in self.bank_readers:
            w += self.bank_readers.pop(bank)
        tok = self.emit(
            "tensor",
            lambda t, a=ps_ap, b=lhsT, c=rhs, s1=start, s2=stop: t.matmul(
                a, b, c, start=s1, stop=s2),
            w, "tS")
        self.bank_last_mm[bank] = tok
        self._record(tok, reads, ())
        return tok

    def transpose(self, ps_ap, in_ap, ident, bank, reads=(), waits=()):
        w = self._collect(reads, (), waits)
        w.append(self.bank_last_mm.get(bank))
        if bank in self.bank_readers:
            w += self.bank_readers.pop(bank)
        tok = self.emit(
            "tensor",
            lambda t, a=ps_ap, b=in_ap, c=ident: t.transpose(a, b, c),
            w, "tS")
        self.bank_last_mm[bank] = tok
        self._record(tok, reads, ())
        return tok

    def op(self, eng, name, *args, reads=(), writes=(), waits=(),
           reads_bank=None, allow_lp=False, **kw):
        sem = {"vector": "vS", "scalar": "aS", "gpsimd": "gS"}[eng]
        w = self._collect(reads, writes, waits)
        if reads_bank is not None:
            w.append(self.bank_last_mm.get(reads_bank))
        if allow_lp:
            nc_ = self.nc

            def bl(e, n=name, a=args, k=kw):
                with nc_.allow_low_precision(reason="bf16 matmul operand"):
                    return getattr(e, n)(*a, **k)
        else:
            bl = lambda e, n=name, a=args, k=kw: getattr(e, n)(*a, **k)
        bl._tag = name
        tok = self.emit(eng, bl, w, sem)
        if reads_bank is not None:
            self.bank_readers.setdefault(reads_bank, []).append(tok)
        self._record(tok, reads, writes)
        return tok

    def dma(self, queue, out, in_, reads=(), writes=(), waits=(), **kw):
        base = {"sync": "qS", "gpsimd": "qG", "scalar": "qA"}[queue]
        n = self._qn.get(queue, 0)
        self._qn[queue] = n + 1
        sem = f"{base}{n % 2}"
        w = self._collect(reads, writes, waits)
        w.append((sem, self.cnt[sem]) if self.cnt[sem] else None)
        tok = self.emit(
            queue,
            lambda e, o=out, i=in_, k=kw: e.dma_start(out=o, in_=i, **k),
            w, (sem, 16))
        self._record(tok, reads, writes)
        return tok

    def coll(self, kind, op, ins, outs, waits=()):
        return self.emit(
            "gpsimd",
            lambda g, k=kind, o=op, a=ins, b=outs: g.collective_compute(
                k, o, replica_groups=[list(range(NCORES))], ins=a, outs=b),
            waits, "cc")

    def wait(self, eng, waits):
        ws = tuple(w for w in waits if w is not None)
        if ws:
            self.ops[eng].append((ws, None, []))

    def barrier_tokens(self):
        return [(s, self.cnt[s]) for s in self.SEMS if self.cnt[s] > 0]

    def build_block(self, ctx):
        for s in self.SEMS:
            self.sems[s] = ctx.enter_context(self.nc.semaphore(s))
        block = ctx.enter_context(self.nc.Block())
        for eng in ENGINES:
            self._replay(block, eng)

    def _replay(self, block, eng):
        ops = self.ops[eng]
        sems = self.sems

        def run(engine):
            seen = {}
            for idx, (waits, build, incs) in enumerate(ops):
                todo = []
                for (s, v) in waits:
                    if seen.get(s, 0) < v:
                        todo.append((s, v))
                        seen[s] = v
                if build is None:
                    for s, v in todo:
                        engine.wait_ge(sems[s], v)
                    continue
                for s, v in todo[1:]:
                    engine.wait_ge(sems[s], v)
                try:
                    inst = build(engine)
                except Exception as ex:
                    raise RuntimeError(
                        f"emit failure on {eng} op#{idx}: "
                        f"{getattr(build, '_tag', '?')}: {ex}") from ex
                if todo:
                    s, v = todo[0]
                    inst._wait_ge(sems[s], v)
                for s, n in incs:
                    inst.then_inc(sems[s], n)

        getattr(block, eng)(run)


def mm_groups(K, groups):
    nk = max(len(g["chunks"]) for g in groups)
    for g in groups:
        g["_w"] = KB._collect(g.get("reads", ()), (), g.get("waits", ()))
        g["_t"] = None
    for ki in range(nk):
        for g in groups:
            if ki >= len(g["chunks"]):
                continue
            lhsT, rhs = g["chunks"][ki]
            start = (ki == 0) and g.get("first_start", True)
            stop = ki == len(g["chunks"]) - 1
            g["_t"] = K.mm(g["ps"], lhsT, rhs, g["bank"], start=start,
                           stop=stop, waits=g["_w"] if ki == 0 else ())
    for g in groups:
        for d in g.get("reads", ()):
            d.did_read(g["_t"])
        if g.get("evac"):
            g["evac"](K)



def build_program(upto=None, dbg=False):
    if upto is None:
        upto = UPTO
    S = SL * NCORES
    T = B * SL
    NC = D // 128
    TW = min(512, T)
    NTT = T // TW
    KTOT = S
    NKT = KTOT // 128
    NHP = NH // 2
    JJ = SL // 128

    nc = bass.Bass("TRN2", target_bir_lowering=False, debug=False)
    K = KB(nc)

    def din(name, shape, dt=F32):
        return nc.dram_tensor(name, shape, dt, kind="ExternalInput").ap()

    def dint(name, shape, dt=F32):
        return nc.dram_tensor(name, shape, dt, kind="Internal").ap()

    def sbuf(name, shape, dt):
        return nc.alloc_sbuf_tensor(name, shape, dt)

    xin = din("xin", [D, T])
    xout = nc.dram_tensor("xout", [D, T], F32, kind="ExternalOutput").ap()
    wqT = din("wqT", [D, D])
    wkT = din("wkT", [D, D])
    wvT = din("wvT", [D, D])
    woT = din("woT", [D, D])
    bqk = din("bqk", [4, D])

    agkv_in = dint("agkv_in", [2 * D * T], BF16)
    agkv_out = dint("agkv_out", [NCORES * 2 * D * T], BF16)
    agK_in = agkv_in[0:D * T].rearrange("(d t) -> d t", d=D)
    agV_in = agkv_in[D * T:2 * D * T].rearrange("(t d) -> t d", d=D)
    agK_out = agkv_out.rearrange("(r a d t) -> r a d t", r=NCORES, a=2, d=D)
    agV_out = agkv_out.rearrange("(r a t d) -> r a t d", r=NCORES, a=2, t=T)

    NKS_ = max(1, KTOT // 1024)
    KVW = B * (KTOT // NKS_)
    WACOLS = max(NC * 512, 2 * KVW)
    B2COLS = max(NC * T, (min(1024, MOE_H) // 128) * T)
    B1COLS = max(NC * T, (GI // 128) * TW, 12 * TW)
    ST = sbuf("ST", [128, NC * T], F32)
    B1 = sbuf("B1", [128, B1COLS], BF16)
    B2 = sbuf("B2", [128, B2COLS], BF16)
    F1 = sbuf("F1", [128, 8 * TW], F32)
    _wa_off = nc.sbuf_base
    WA = sbuf("WA", [128, WACOLS], BF16)
    SCN = nc.alloc_sbuf_tensor_at("SCN", [128, 3 * TW], F32,
                              offset=_wa_off)
    _wbu_off = nc.sbuf_base
    WBU = sbuf("WBU", [128, WACOLS], BF16)
    ITMP2 = nc.alloc_sbuf_tensor_at("ITMP2", [128, 512], I32,
                                    offset=_wbu_off)
    RW = sbuf("RW", [128, 3 * TW], F32)
    SM = sbuf("SM", [128, 688], F32)
    SMB = sbuf("SMB", [128, 128], BF16)
    ITMP = nc.alloc_sbuf_tensor("ITMP", [128, 512], I32)
    MSKS = sbuf("MSKS", [9, T], BF16)
    MSKB = sbuf("MSKB", [128, 1024], BF16)
    B2S = sbuf("B2S", [9, D], BF16)
    SEL = sbuf("SEL", [16, 16 * 128], BF16)
    XRT = sbuf("XRT", [NST, T], F32)

    dST = Dep()
    dB1 = Dep()
    dB2 = Dep()
    dB3 = Dep()
    dF1 = Dep()
    dWA = Dep()
    dWB = Dep()
    dRW = Dep()
    dSM = Dep()
    dSMB = Dep()
    dAGin = Dep()
    dAGout = Dep()
    dSEL = Dep()
    dMS = Dep()
    dMB = Dep()
    dB2S = Dep()

    PS = [nc.alloc_psum_tensor(f"ps{i}", [128, 512], F32) for i in range(8)]

    def STc(k):
        return ST[:, k * T:(k + 1) * T]

    def B1c(k):
        return B1[:, k * T:(k + 1) * T]

    def B2c(k):
        return B2[:, k * T:(k + 1) * T]

    IDENT = SM[:, 0:128]
    ONES = SM[:, 128:256]

    def bias_col(j, k):
        return SM[:, 256 + j * NC + k: 256 + j * NC + k + 1]

    def phase_barrier():
        toks = K.barrier_tokens()
        for e_ in ENGINES:
            K.wait(e_, toks)

    _consts = {}

    def constcol(val):
        if val not in _consts:
            idx = 664 + len(_consts)
            K.op("gpsimd", "memset", SM[:, idx:idx + 1], float(val),
                 writes=[dSM])
            _consts[val] = idx
        i = _consts[val]
        return SM[:, i:i + 1]

    K.dma("sync", out=ST[:].rearrange("p (c t) -> p c t", c=NC),
          in_=xin.rearrange("(c p) t -> p c t", p=128), writes=[dST])
    K.dma("sync", out=SM[:, 256:256 + 4 * NC].rearrange(
        "p (j k) -> p j k", j=4),
        in_=bqk.rearrange("j (k p) -> p j k", p=128), writes=[dSM],
        allow_slow_non_contiguous=True)

    t0 = K.op("gpsimd", "memset", SM[:, 0:128], 0.0, writes=[dSM])
    t0 = K.op("gpsimd", "affine_select", out=SM[:, 0:128], in_=SM[:, 0:128],
              compare_op=ALU.not_equal, fill=1.0, base=0,
              pattern=[[-1, 128]], channel_multiplier=1,
              waits=[t0], writes=[dSM])
    t1 = K.op("gpsimd", "memset", SMB[:, 0:128], 0.0, writes=[dSMB])
    t1 = K.op("gpsimd", "affine_select", out=SMB[:, 0:128], in_=SMB[:, 0:128],
              compare_op=ALU.not_equal, fill=1.0, base=0,
              pattern=[[-1, 128]], channel_multiplier=1,
              waits=[t1], writes=[dSMB])
    K.op("gpsimd", "memset", SM[:, 128:256], 1.0, writes=[dSM])

    def layernorm(src_chunk, src_dep, dst_chunk, dst_dep, gamma_col=None,
                  rms=False, eps=EPS_LN, tile_cb=None):
        last = None
        for tt in range(NTT):
            sl = slice(tt * TW, (tt + 1) * TW)
            gsum, gsq = [], []
            for k in range(NC):
                sq = F1[:, k * TW:(k + 1) * TW]
                K.op("scalar", "activation", sq, src_chunk(k)[:, sl],
                     AF.Square, reads=[src_dep], writes=[dF1])
                gsq.append((ONES[:, 0:1], sq))
                if not rms:
                    gsum.append((ONES[:, 0:1], src_chunk(k)[:, sl]))
            groups = [dict(bank=6, ps=PS[6][0:1, 0:TW], chunks=gsq,
                           reads=[dF1, dSM])]
            if not rms:
                groups.append(dict(bank=7, ps=PS[7][0:1, 0:TW], chunks=gsum,
                                   reads=[src_dep, dSM]))
            mm_groups(K, groups)
            mrow = RW[0:1, 0:TW]
            rrow = RW[0:1, TW:2 * TW]
            trow = RW[0:1, 2 * TW:3 * TW]
            if not rms:
                K.op("vector", "tensor_scalar", mrow, PS[7][0:1, 0:TW],
                     1.0 / D, None, ALU.mult, reads_bank=7, writes=[dRW])
                K.op("vector", "tensor_scalar", rrow, PS[6][0:1, 0:TW],
                     1.0 / D, None, ALU.mult, reads_bank=6, writes=[dRW])
                K.op("vector", "tensor_tensor", trow, mrow, mrow,
                     ALU.mult, reads=[dRW], writes=[dRW])
                K.op("vector", "tensor_tensor", rrow, rrow, trow,
                     ALU.subtract, reads=[dRW], writes=[dRW])
            else:
                K.op("vector", "tensor_scalar", rrow, PS[6][0:1, 0:TW],
                     1.0 / D, None, ALU.mult, reads_bank=6, writes=[dRW])
            K.op("scalar", "activation", rrow, rrow, AF.Ln,
                 bias=constcol(eps)[0:1, :], reads=[dRW, dSM], writes=[dRW])
            K.op("scalar", "activation", rrow, rrow, AF.Exp,
                 scale=-0.5, reads=[dRW], writes=[dRW])
            K.mm(PS[5][:, 0:TW], ONES[0:1, 0:128], rrow, 5,
                 start=True, stop=True, reads=[dRW, dSM])
            if not rms:
                K.mm(PS[4][:, 0:TW], ONES[0:1, 0:128], mrow, 4,
                     start=True, stop=True, reads=[dRW, dSM])
            if tile_cb is not None:
                tile_cb(tt)
            for k in range(NC):
                tmp = F1[:, (k % 2) * TW:((k % 2) + 1) * TW]
                if not rms:
                    K.op("vector", "tensor_tensor", tmp,
                         src_chunk(k)[:, sl], PS[4][:, 0:TW], ALU.subtract,
                         reads=[src_dep], writes=[dF1], reads_bank=4)
                    last = K.op("vector", "tensor_tensor",
                                dst_chunk(k)[:, sl], tmp, PS[5][:, 0:TW],
                                ALU.mult, reads=[dF1], writes=[dst_dep],
                                reads_bank=5)
                else:
                    K.op("vector", "tensor_tensor", tmp,
                         src_chunk(k)[:, sl], PS[5][:, 0:TW], ALU.mult,
                         reads=[src_dep], writes=[dF1], reads_bank=5)
                    last = K.op("vector", "tensor_scalar",
                                dst_chunk(k)[:, sl], tmp, gamma_col(k), None,
                                ALU.mult, reads=[dF1, dSM],
                                writes=[dst_dep])
        return last

    def dense_fm(wT_dram, n_out, src_chunk, src_dep, evac_fn):
        wb = [WA, WBU]
        wdep = [dWA, dWB]
        HWO = min(512, n_out)
        nhalf = n_out // HWO
        pend = []

        def flush():
            while pend:
                pair, pend[:] = pend[:2], pend[2:]
                mm_groups(K, pair)

        for half in range(nhalf):
            buf = wb[half % 2]
            bd = wdep[half % 2]
            ldt = K.dma("gpsimd",
                        out=buf[:, 0:NC * HWO]
                        .rearrange("p (c o) -> p c o", c=NC),
                        in_=wT_dram[:, half * HWO:(half + 1) * HWO]
                        .rearrange("(c p) o -> p c o", p=128),
                        writes=[bd])
            for oc in range(HWO // 128):
                o_glob = half * (HWO // 128) + oc
                for tt in range(NTT):
                    bank = (o_glob * NTT + tt) % 2
                    chunks = []
                    for k in range(NC):
                        lhsT = buf[:, k * HWO + oc * 128:
                                   k * HWO + oc * 128 + 128]
                        rhs = src_chunk(k)[:, tt * TW:(tt + 1) * TW]
                        chunks.append((lhsT, rhs))
                    pend.append(dict(
                        bank=bank, ps=PS[bank][:, 0:TW], chunks=chunks,
                        reads=[bd, src_dep], waits=[ldt],
                        evac=(lambda KK, o=o_glob, t_=tt, b_=bank:
                              evac_fn(KK, o, t_, b_, PS[b_][:, 0:TW]))))
                    if len(pend) == 2:
                        flush()
        flush()

    def evac_act_bias(dst_chunk, dst_dep, bias_j, act=AF.Identity):
        def f(KK, o, tt, bank, ps):
            KK.op("scalar", "activation",
                  dst_chunk(o)[:, tt * TW:(tt + 1) * TW], ps, act,
                  bias=bias_col(bias_j, o), reads=[dSM], writes=[dst_dep],
                  reads_bank=bank)
        return f

    layernorm(STc, dST, B1c, dB1)

    if upto >= 2:
        dense_fm(wkT, D, B1c, dB1, evac_act_bias(B2c, dB2, 1))
        stK = K.dma("sync", out=agK_in.rearrange("(c p) t -> p c t", p=128),
                    in_=B2[:, 0:NC * T].rearrange("p (c t) -> p c t", c=NC),
                    reads=[dB2], writes=[dAGin])
        v_toks = []

        def evac_v(KK, o, tt, bank, ps):
            fsl = F1[:, 4 * TW:5 * TW]
            KK.op("scalar", "activation", fsl, ps, AF.Identity,
                  bias=bias_col(2, o), reads=[dSM], writes=[dF1],
                  reads_bank=bank)
            for jj_ in range(TW // 128):
                tb = 2 + (jj_ % 2)
                KK.transpose(PS[tb][:, 0:128],
                             fsl[:, jj_ * 128:(jj_ + 1) * 128],
                             IDENT, tb, reads=[dF1, dSM])
                KK.op("vector", "tensor_copy",
                      MSKB[:, jj_ * 128:(jj_ + 1) * 128],
                      PS[tb][:, 0:128], reads_bank=tb, writes=[dMB])
            v_toks.append(KK.dma(
                "gpsimd",
                out=agV_in[tt * TW:(tt + 1) * TW, o * 128:(o + 1) * 128]
                .rearrange("(jj q) e -> q jj e", q=128),
                in_=MSKB[:, 0:TW].rearrange("p (jj e) -> p jj e", e=128),
                reads=[dMB], writes=[dAGin]))

        dense_fm(wvT, D, B1c, dB1, evac_v)
        stV = v_toks[-1]
        ag_tok = K.coll("AllGather", ALU.bypass, [agkv_in.opt()],
                        [agkv_out.opt()], waits=[stK, stV] + dAGout.for_write())
        dAGout.did_write(ag_tok)
        dense_fm(wqT, D, B1c, dB1, evac_act_bias(B2c, dB2, 0))

    def attention():
        NKS = max(1, KTOT // 512)
        KSW = KTOT // NKS
        RPS = max(1, KSW // SL)
        NJ = KSW // 128
        KHALF = B * KSW
        NHALF = max(1, min(2, KVW // KHALF))
        dKh = [Dep(), Dep()]
        dVh = [Dep(), Dep()]
        dHA = Dep()
        WAK = WACOLS - KVW
        HA = F1[:, 4 * TW:4 * TW + T]
        dPLp = [Dep(), Dep()]
        dMCp = [Dep(), Dep()]
        dZCp = [Dep(), Dep()]
        dATp = [Dep(), Dep()]
        assert WAK >= 8 * SL, (WAK, SL)
        assert NHALF * KHALF <= KVW
        _bt = K.barrier_tokens()
        for _d in dPLp + dMCp + dZCp + dATp + dKh + dVh:
            _d.r = list(_bt)
        for hp in range(NHP):
            for ks in range(NKS):
                kp = ks % NHALF
                KOFF = WAK + kp * KHALF
                ldk, ldv = [], []
                for b in range(B):
                    ldk.append(K.dma(
                        "sync",
                        out=WA[:, KOFF + b * KSW:KOFF + (b + 1) * KSW]
                        .rearrange("p (rr s) -> p rr s", rr=RPS),
                        in_=agK_out[ks * RPS:(ks + 1) * RPS, 0,
                                    hp * 128:(hp + 1) * 128,
                                    b * SL:(b + 1) * SL]
                        .rearrange("r d s -> d r s"),
                        writes=[dKh[kp]], reads=[dAGout]))
                    vtoks = []
                    for rr in range(RPS):
                        vtoks.append(K.dma(
                            "sync",
                            out=WBU[:, KOFF + b * KSW + rr * JJ * 128:
                                    KOFF + b * KSW + (rr + 1) * JJ * 128]
                            .rearrange("p (jj e) -> p jj e", jj=JJ),
                            in_=agV_out[ks * RPS + rr, 1,
                                        b * SL:(b + 1) * SL,
                                        hp * 128:(hp + 1) * 128]
                            .rearrange("(jj q) e -> q jj e", jj=JJ),
                            writes=[dVh[kp]], reads=[dAGout]))
                    ldv.append(vtoks[-1])
                def s1mm(h2, j, up):
                    prow = slice(h2 * 64, h2 * 64 + 64)
                    for b in range(B):
                        lhsT = WA[prow, KOFF + b * KSW + j * 128:
                                  KOFF + b * KSW + j * 128 + 128]
                        rhs = B2[prow, hp * T + b * SL:
                                 hp * T + (b + 1) * SL]
                        K.mm(PS[4 + b][:, 0:SL], lhsT, rhs, 4 + b,
                             start=True, stop=True,
                             reads=[dKh[kp], dB2], waits=[ldk[b]])

                def s1ev(h2, j, up):
                    PL = WA[:, up * 4 * SL:up * 4 * SL + 4 * SL]
                    for b in range(B):
                        K.op("vector", "tensor_copy",
                             PL[:, b * SL:(b + 1) * SL],
                             PS[4 + b][:, 0:SL],
                             reads_bank=4 + b, writes=[dPLp[up]])

                def s2a(h2, j, up):
                    PL = WA[:, up * 4 * SL:up * 4 * SL + 4 * SL]
                    MC = MSKB[:, up * SL:(up + 1) * SL]
                    AT = WBU[:, up * 4 * SL:up * 4 * SL + 4 * SL]
                    dPL = dPLp[up]
                    dMC = dMCp[up]
                    dAT = dATp[up]
                    K.op("vector", "tensor_tensor", MC,
                         PL[:, 0:SL], PL[:, SL:2 * SL], ALU.max,
                         reads=[dPL], writes=[dMC])
                    K.op("vector", "tensor_tensor", MC,
                         MC, PL[:, 2 * SL:3 * SL], ALU.max,
                         reads=[dPL, dMC], writes=[dMC])
                    K.op("vector", "tensor_tensor", MC,
                         MC, PL[:, 3 * SL:4 * SL], ALU.max,
                         reads=[dPL, dMC], writes=[dMC])
                    K.op("vector", "tensor_tensor", PL, PL,
                         MC.unsqueeze(1).broadcast_to([128, 4, SL]),
                         ALU.subtract,
                         reads=[dMC, dPL], writes=[dPL])
                    K.op("scalar", "activation", AT, PL, AF.Exp,
                         reads=[dPL], writes=[dAT])

                def s2b(h2, j, up):
                    prow = slice(h2 * 64, h2 * 64 + 64)
                    PL = WA[:, up * 4 * SL:up * 4 * SL + 4 * SL]
                    MC = MSKB[:, up * SL:(up + 1) * SL]
                    ZC = RW[:, up * SL:(up + 1) * SL]
                    AT = WBU[:, up * 4 * SL:up * 4 * SL + 4 * SL]
                    dPL = dPLp[up]
                    dMC = dMCp[up]
                    dZC = dZCp[up]
                    dAT = dATp[up]
                    K.op("vector", "tensor_tensor", ZC,
                         AT[:, 0:SL], AT[:, SL:2 * SL], ALU.add,
                         reads=[dAT], writes=[dZC])
                    K.op("vector", "tensor_tensor", ZC, ZC,
                         AT[:, 2 * SL:3 * SL], ALU.add,
                         reads=[dAT, dZC], writes=[dZC])
                    K.op("vector", "tensor_tensor", ZC, ZC,
                         AT[:, 3 * SL:4 * SL], ALU.add,
                         reads=[dAT, dZC], writes=[dZC])
                    K.op("scalar", "activation", MC, ZC, AF.Ln,
                         reads=[dZC], writes=[dMC])
                    K.op("vector", "tensor_tensor", PL, PL,
                         MC.unsqueeze(1).broadcast_to([128, 4, SL]),
                         ALU.subtract,
                         reads=[dMC, dPL], writes=[dPL])
                    ta = K.op("scalar", "activation", AT, PL, AF.Exp,
                              reads=[dPL], writes=[dAT])
                    for b in range(B):
                        off = (KOFF + b * KSW + (j // JJ) * (JJ * 128)
                               + (j % JJ) * 128 + h2 * 64)
                        lhsT = WBU[:, off:off + 64]
                        K.mm(PS[b][prow, 0:SL], lhsT,
                             AT[:, b * SL:(b + 1) * SL], b,
                             start=(j == 0), stop=(j == NJ - 1),
                             reads=[dVh[kp], dATp[up]],
                             waits=[ldv[b], ta])

                units = [(h2, j) for h2 in range(2) for j in range(NJ)]
                nu = len(units)
                for ui in range(nu):
                    s1mm(units[ui][0], units[ui][1], ui % 2)
                    if ui >= 1:
                        u1 = units[ui - 1]
                        s2a(u1[0], u1[1], (ui - 1) % 2)
                    if ui >= 2:
                        u2 = units[ui - 2]
                        s2b(u2[0], u2[1], (ui - 2) % 2)
                    s1ev(units[ui][0], units[ui][1], ui % 2)
                s2a(units[nu - 1][0], units[nu - 1][1], (nu - 1) % 2)
                if nu >= 2:
                    s2b(units[nu - 2][0], units[nu - 2][1], (nu - 2) % 2)
                s2b(units[nu - 1][0], units[nu - 1][1], (nu - 1) % 2)
                for h2 in range(2):
                    prow = slice(h2 * 64, h2 * 64 + 64)
                    for b in range(B):
                        dst = HA[prow, b * SL:(b + 1) * SL]
                        if ks == 0:
                            K.op("vector", "tensor_copy", dst,
                                 PS[b][prow, 0:SL], reads_bank=b,
                                 writes=[dHA])
                        else:
                            K.op("vector", "tensor_tensor", dst, dst,
                                 PS[b][prow, 0:SL], ALU.add,
                                 reads=[dHA], reads_bank=b, writes=[dHA])
            K.op("vector", "tensor_copy", B1c(hp), HA[:, 0:T],
                 reads=[dHA], writes=[dB1])

    if upto >= 3:
        attention()
        phase_barrier()

        def evac_wo(KK, o, tt, bank, ps):
            KK.op("vector", "scalar_tensor_tensor",
                  STc(o)[:, tt * TW:(tt + 1) * TW],
                  ps, bias_col(3, o),
                  STc(o)[:, tt * TW:(tt + 1) * TW],
                  ALU.add, ALU.add,
                  reads=[dSM, dST], writes=[dST], reads_bank=bank)

        dense_fm(woT, D, B1c, dB1, evac_wo)

    if upto >= 4:
        HSW = min(1024, MOE_H)
        NHS = MOE_H // HSW
        HCS = HSW // 128
        NT128 = T // 128
        moe_w1 = din("moe_w1", [MOE_E, D, MOE_H])
        moe_w2 = din("moe_w2", [MOE_E, MOE_H, D])
        moe_b1 = din("moe_b1", [MOE_E, MOE_H])
        moe_b2 = din("moe_b2", [MOE_E, D])
        wgrT = din("wgrT", [D, NST])
        wg_in = din("wg", [MOE_E, NST])
        st_spill = dint("st_spill", [D, T])
        dSP = Dep()

        phase_barrier()
        WGR32 = SM[:, 532:532 + NC * NST]
        ld_wgr32 = K.dma("sync", out=WGR32.rearrange("p (c g) -> p c g",
                                                     c=NC),
                         in_=wgrT.rearrange("(c p) g -> p c g", p=128),
                         writes=[dSM])
        WGS = SM[0:NST, 676:677]
        gw = [(WGR32[:, k * NST:(k + 1) * NST], ONES[:, 0:1])
              for k in range(NC)]
        mm_groups(K, [dict(bank=3, ps=PS[3][0:NST, 0:1], chunks=gw,
                           reads=[dSM], waits=[ld_wgr32])])
        K.op("vector", "tensor_copy", WGS, PS[3][0:NST, 0:1], reads_bank=3,
             writes=[dSM])
        XR = XRT[:, :]
        dXR = Dep()

        def gate_cb(tt):
            sl_ = slice(tt * TW, (tt + 1) * TW)
            chunks = [(WGR32[:, k * NST:(k + 1) * NST], STc(k)[:, sl_])
                      for k in range(NC)]
            mm_groups(K, [dict(bank=3, ps=PS[3][0:NST, 0:TW], chunks=chunks,
                               reads=[dSM, dST])])
            tmp = RW[0:NST, 0:TW]
            K.op("vector", "tensor_scalar", tmp, PS[4][0:NST, 0:TW], WGS,
                 None, ALU.mult, reads=[dSM], reads_bank=4, writes=[dRW])
            K.op("vector", "tensor_tensor", tmp, PS[3][0:NST, 0:TW], tmp,
                 ALU.subtract, reads=[dRW], writes=[dRW], reads_bank=3)
            K.op("vector", "tensor_tensor", XR[:, sl_], tmp,
                 PS[5][0:NST, 0:TW], ALU.mult, reads=[dRW], writes=[dXR],
                 reads_bank=5)

        tsel = K.op("gpsimd", "memset", SEL[:, :], 1.0, writes=[dSEL])
        tsel = K.op("gpsimd", "affine_select", out=SEL[:, :], in_=SEL[:, :],
                    compare_op=ALU.is_ge, fill=0.0, base=0,
                    channel_multiplier=-128, pattern=[[1, 16 * 128]],
                    waits=[tsel], writes=[dSEL])
        tsel = K.op("gpsimd", "affine_select", out=SEL[:, :], in_=SEL[:, :],
                    compare_op=ALU.is_ge, fill=0.0, base=127,
                    channel_multiplier=128, pattern=[[-1, 16 * 128]],
                    waits=[tsel], writes=[dSEL])
        layernorm(STc, dST, B1c, dB1, tile_cb=gate_cb)
        tsp = K.dma("sync", out=st_spill.rearrange("(c p) t -> p c t", p=128),
                    in_=ST[:].rearrange("p (c t) -> p c t", c=NC),
                    reads=[dST], writes=[dSP])
        tz = K.op("vector", "memset", ST[:], 0.0, waits=[tsp], writes=[dST])

        ld_wg = K.dma("sync", out=SM[0:MOE_E, 544:544 + NST], in_=wg_in,
                      writes=[dSM])
        ld_b1 = K.dma("sync", out=SM[:, 288:288 + MOE_E * (MOE_H // 128)]
                      .rearrange("p (e k) -> p e k", e=MOE_E),
                      in_=moe_b1.rearrange("e (k p) -> p e k", p=128),
                      writes=[dSM], allow_slow_non_contiguous=True)
        ld_b2 = K.dma("gpsimd", out=B2S[1:9, :], in_=moe_b2, writes=[dB2S])
        ld_b2 = K.op("gpsimd", "memset", B2S[0:1, :], 0.0, waits=[ld_b2],
                     writes=[dB2S])

        def b1_col(e, hk):
            c = 288 + e * (MOE_H // 128) + hk
            return SM[:, c:c + 1]

        WGB = SM[0:MOE_E, 544:544 + NST]
        WGU = SM[0:MOE_E, 560:560 + NST]
        WGN = SM[0:MOE_E, 576:577]
        K.op("scalar", "activation", WGU, WGB, AF.Square,
             reads=[dSM], writes=[dSM])
        K.op("vector", "tensor_reduce", WGN, WGU, AX.X, ALU.add,
             reads=[dSM], writes=[dSM])
        K.op("scalar", "activation", WGN, WGN, AF.Ln, reads=[dSM],
             writes=[dSM])
        K.op("scalar", "activation", WGN, WGN, AF.Exp, scale=0.5,
             reads=[dSM], writes=[dSM])
        K.op("vector", "tensor_scalar", WGN, WGN, float(EPS_GATE), None,
             ALU.max, reads=[dSM], writes=[dSM])
        K.op("vector", "reciprocal", WGN, WGN, reads=[dSM], writes=[dSM])
        K.op("vector", "tensor_scalar", WGU, WGB, WGN, None, ALU.mult,
             reads=[dSM], writes=[dSM])
        WGUT = SM[0:NST, 600:600 + MOE_E]
        K.transpose(PS[5][0:NST, 0:MOE_E], WGU, IDENT[0:MOE_E, 0:MOE_E], 5,
                    reads=[dSM])
        K.op("vector", "tensor_copy", WGUT, PS[5][0:NST, 0:MOE_E],
             reads_bank=5, writes=[dSM])
        GE = SM[:, 608:608 + MOE_E]
        NM = SM[:, 616:617]
        ZS = SM[:, 617:618]
        IDX = SM[:, 619:620]
        SCC = SM[:, 620:621]
        MK8 = SM[:, 621:621 + MOE_E]
        MK9 = SM[:, 620:629]
        IOTA8 = SM[:, 650:650 + MOE_E]
        K.op("gpsimd", "iota", IOTA8, pattern=[[1, MOE_E]], base=0,
             channel_multiplier=0, allow_small_or_imprecise_dtypes=True,
             writes=[dSM])
        for jt in range(NT128):
            bk = 4 + jt % 2
            K.mm(PS[bk][0:128, 0:MOE_E], XR[:, jt * 128:(jt + 1) * 128],
                 WGUT, bk, start=True, stop=True, reads=[dXR, dSM])
            MX = SM[:, 632:633]
            K.op("vector", "tensor_reduce", MX, PS[bk][0:128, 0:MOE_E],
                 AX.X, ALU.max, reads_bank=bk, writes=[dSM])
            K.op("vector", "tensor_scalar", NM, MX, -1.0, None, ALU.mult,
                 reads=[dSM], writes=[dSM])
            K.op("scalar", "activation", GE, PS[bk][0:128, 0:MOE_E],
                 AF.Exp, bias=NM, reads=[dSM], reads_bank=bk, writes=[dSM])
            K.op("vector", "tensor_reduce", ZS, GE, AX.X, ALU.add,
                 reads=[dSM], writes=[dSM])
            K.op("vector", "reciprocal", SCC, ZS, reads=[dSM], writes=[dSM])
            EQ = SM[:, 633:633 + MOE_E]
            K.op("vector", "tensor_scalar", EQ, PS[bk][0:128, 0:MOE_E],
                 MX, None, ALU.is_equal, reads=[dSM], reads_bank=bk,
                 writes=[dSM])
            K.op("vector", "tensor_scalar", EQ, EQ, -1e9, 1e9, ALU.mult,
                 ALU.add, reads=[dSM], writes=[dSM])
            K.op("vector", "tensor_tensor", EQ, EQ, IOTA8, ALU.add,
                 reads=[dSM], writes=[dSM])
            K.op("vector", "tensor_reduce", IDX, EQ, AX.X, ALU.min,
                 reads=[dSM], writes=[dSM])
            K.op("vector", "tensor_scalar", MK8, IOTA8, IDX, None,
                 ALU.is_equal, reads=[dSM], writes=[dSM])
            K.transpose(PS[2][0:9, 0:128], MK9, IDENT, 2, reads=[dSM])
            K.op("vector", "tensor_copy",
                 MSKS[0:9, jt * 128:(jt + 1) * 128],
                 PS[2][0:9, 0:128], reads_bank=2, writes=[dMS])
            K.op("vector", "tensor_copy",
                 XRT[0:1, jt * 128:(jt + 1) * 128],
                 PS[2][0:1, 0:128], reads_bank=2, writes=[dXR])

        for e in range(MOE_E):
            for hs in range(NHS):
                wbuf, wdep = (WA, dWA) if hs % 2 == 0 else (WBU, dWB)
                hreg = 0
                ldw1 = K.dma("gpsimd",
                             out=wbuf[:, 0:NC * HSW]
                             .rearrange("p (c h) -> p c h", c=NC),
                             in_=moe_w1[e, :, hs * HSW:(hs + 1) * HSW]
                             .rearrange("(c p) h -> p c h", p=128),
                             writes=[wdep])
                pend = []
                for hc in range(HCS):
                    for tt in range(NTT):
                        bank = (hc * NTT + tt) % 4
                        chunks = [(wbuf[:, k * HSW + hc * 128:
                                        k * HSW + hc * 128 + 128],
                                   B1c(k)[:, tt * TW:(tt + 1) * TW])
                                  for k in range(NC)]

                        def ev_h(KK, hc=hc, tt=tt, bank=bank, hs=hs,
                                 hreg=hreg, e=e):
                            KK.op("scalar", "activation",
                                  B2[:, hreg + hc * T + tt * TW:
                                     hreg + hc * T + (tt + 1) * TW],
                                  PS[bank][:, 0:TW], AF.Gelu,
                                  bias=b1_col(e, hs * HCS + hc),
                                  reads=[dSM], writes=[dB2],
                                  reads_bank=bank)
                        pend.append(dict(bank=bank, ps=PS[bank][:, 0:TW],
                                         chunks=chunks, reads=[wdep, dB1],
                                         waits=[ldw1], evac=ev_h))
                        if len(pend) == 4:
                            mm_groups(K, pend)
                            pend = []
                if pend:
                    mm_groups(K, pend)
                wb2, wd2 = (WBU, dWB) if hs % 2 == 0 else (WA, dWA)
                ldw2 = K.dma("gpsimd",
                             out=wb2[:, 0:HCS * D]
                             .rearrange("p (c o) -> p c o", c=HCS),
                             in_=moe_w2[e, hs * HSW:(hs + 1) * HSW, :]
                             .rearrange("(c p) o -> p c o", p=128),
                             writes=[wd2])
                pend = []
                for tt in range(NTT):
                    bk = 7
                    K.mm(PS[bk][:, 0:TW],
                         SEL[0:9, (e + 1) * 128:(e + 2) * 128],
                         MSKS[0:9, tt * TW:(tt + 1) * TW], bk,
                         start=True, stop=True, reads=[dMS, dSEL])
                    K.op("vector", "tensor_copy", MSKB[:, 0:TW],
                         PS[bk][:, 0:TW], reads_bank=bk, writes=[dMB])
                    for oc in range(NC):
                        bank = 4 + (oc + tt) % 3
                        chunks = [(wb2[:, c * D + oc * 128:
                                       c * D + oc * 128 + 128],
                                   B2[:, hreg + c * T + tt * TW:
                                      hreg + c * T + (tt + 1) * TW])
                                  for c in range(HCS)]

                        def ev_o(KK, oc=oc, tt=tt, bank=bank):
                            tmp = F1[:, 0:TW]
                            KK.op("vector", "tensor_tensor", tmp,
                                  PS[bank][:, 0:TW],
                                  MSKB[:, 0:TW], ALU.mult,
                                  reads=[dMB], writes=[dF1],
                                  reads_bank=bank)
                            KK.op("vector", "tensor_tensor",
                                  STc(oc)[:, tt * TW:(tt + 1) * TW],
                                  STc(oc)[:, tt * TW:(tt + 1) * TW],
                                  tmp, ALU.add, reads=[dF1, dST],
                                  writes=[dST])
                        pend.append(dict(bank=bank, ps=PS[bank][:, 0:TW],
                                         chunks=chunks, reads=[wd2, dB2],
                                         waits=[ldw2], evac=ev_o))
                        if len(pend) == 3:
                            mm_groups(K, pend)
                            pend = []
                    if pend:
                        mm_groups(K, pend)
                        pend = []

        for tt in range(NTT):
            bk = 6 + tt % 2
            K.mm(PS[bk][:, 0:TW], ONES[0:1, 0:128],
                 XRT[0:1, tt * TW:(tt + 1) * TW], bk, start=True, stop=True,
                 reads=[dXR, dSM])
            K.op("vector", "tensor_copy", MSKB[:, 0:TW],
                 PS[bk][:, 0:TW], reads_bank=bk, writes=[dMB])
            lds = K.dma("sync",
                        out=F1[:, 0:NC * TW]
                        .rearrange("p (c w) -> p c w", c=NC),
                        in_=st_spill.rearrange("(c p) t -> p c t", p=128)
                        [:, :, tt * TW:(tt + 1) * TW],
                        reads=[dSP], writes=[dF1])
            for oc in range(NC):
                bank = oc % 4
                K.mm(PS[bank][:, 0:TW],
                     B2S[:, oc * 128:(oc + 1) * 128].transpose([0, 1])
                     if False else B2S[0:9, oc * 128:(oc + 1) * 128],
                     MSKS[0:9, tt * TW:(tt + 1) * TW], bank,
                     start=True, stop=True, reads=[dB2S, dMS],
                     waits=[ld_b2])
                sl_ = slice(tt * TW, (tt + 1) * TW)
                t1 = K.op("vector", "tensor_tensor", PS[bank][:, 0:TW]
                          if False else STc(oc)[:, sl_],
                          STc(oc)[:, sl_], PS[bank][:, 0:TW], ALU.add,
                          reads=[dST], writes=[dST], reads_bank=bank)
                t2 = K.op("vector", "tensor_tensor", STc(oc)[:, sl_],
                          STc(oc)[:, sl_], MSKB[:, 0:TW], ALU.mult,
                          reads=[dST, dMB], writes=[dST])
                t3 = K.op("vector", "tensor_tensor", STc(oc)[:, sl_],
                          STc(oc)[:, sl_],
                          F1[:, oc * TW:(oc + 1) * TW], ALU.add,
                          reads=[dST, dF1], writes=[dST], waits=[lds])

    if upto >= 5:
        QW = min(512, SL)
        NQ = SL // QW
        wdtT = din("wdtT", [D, D])
        wBT = din("wBT", [D, NST])
        wCT = din("wCT", [D, NST])
        w1lT = din("w1lT", [D, DFF])
        w2lT = din("w2lT", [DFF, D])
        l1vec = din("l1vec", [6, D])
        l1b1 = din("l1b1", [DFF])
        l1bBC = din("l1bBC", [2, NST])
        l1A = din("l1A", [NST])
        cmask_in = din("cmask", [8])
        y_spill = dint("y_spill", [D, T])
        SUMW = NST * NC
        agm_in = dint("agm_in", [B, 128, 2 * SUMW])
        agm_out = dint("agm_out", [NCORES, B, 128, 2 * SUMW])
        dYS = Dep()
        dAGM = Dep()

        phase_barrier()
        K.dma("sync", out=SM[:, 256:256 + 6 * NC]
              .rearrange("p (j k) -> p j k", j=6),
              in_=l1vec.rearrange("j (k p) -> p j k", p=128),
              writes=[dSM], allow_slow_non_contiguous=True)
        K.dma("sync", out=SM[:, 304:304 + DFF // 128],
              in_=l1b1.rearrange("(k p) -> p k", p=128), writes=[dSM],
              allow_slow_non_contiguous=True)
        K.dma("sync", out=SM[0:NST, 336:338],
              in_=l1bBC.rearrange("j g -> g j"), writes=[dSM],
              allow_slow_non_contiguous=True)
        K.dma("sync", out=SM[0:1, 338:338 + NST],
              in_=l1A.rearrange("(j g) -> j g", j=1), writes=[dSM],
              allow_slow_non_contiguous=False)
        K.dma("sync", out=SM[0:1, 388:396],
              in_=cmask_in.rearrange("(j g) -> j g", j=1), writes=[dSM])

        def l1col(j, k):
            return SM[:, 256 + j * NC + k:256 + j * NC + k + 1]

        def b1lcol(hk):
            return SM[:, 304 + hk:304 + hk + 1]

        K.op("scalar", "activation", SM[0:1, 338:338 + NST],
             SM[0:1, 338:338 + NST], AF.Exp, reads=[dSM], writes=[dSM])
        K.op("vector", "tensor_scalar", SM[0:1, 338:338 + NST],
             SM[0:1, 338:338 + NST], -1.0, None, ALU.mult,
             reads=[dSM], writes=[dSM])
        K.mm(PS[2][:, 0:NST], ONES[0:1, 0:128], SM[0:1, 338:338 + NST], 2,
             start=True, stop=True, reads=[dSM])
        K.op("vector", "tensor_copy", SM[:, 354:354 + NST], PS[2][:, 0:NST],
             reads_bank=2, writes=[dSM])
        K.mm(PS[3][:, 0:8], ONES[0:1, 0:128], SM[0:1, 388:396], 3,
             start=True, stop=True, reads=[dSM])
        K.op("vector", "tensor_copy", SM[:, 396:404], PS[3][:, 0:8],
             reads_bank=3, writes=[dSM])

        def acol(n):
            return SM[:, 354 + n:354 + n + 1]

        layernorm(STc, dST, B1c, dB1, gamma_col=lambda k: l1col(0, k),
                  rms=True, eps=EPS_RMS)
        tsp = K.dma("sync", out=st_spill.rearrange("(c p) t -> p c t", p=128),
                    in_=ST[:].rearrange("p (c t) -> p c t", c=NC),
                    reads=[dST], writes=[dSP])
        K.wait("sync", [tsp])

        C1 = constcol(1.0)

        def evac_softplus(KK, o, tt, bank, ps):
            X = RW[:, 0:TW]
            Yt = RW[:, TW:2 * TW]
            KK.op("scalar", "activation", X, ps, AF.Abs,
                  bias=l1col(4, o), reads=[dSM], writes=[dRW],
                  reads_bank=bank)
            KK.op("scalar", "activation", Yt, X, AF.Exp, scale=-1.0,
                  reads=[dRW], writes=[dRW])
            KK.op("scalar", "activation", Yt, Yt, AF.Ln, bias=C1,
                  reads=[dRW, dSM], writes=[dRW])
            KK.op("scalar", "activation", X, ps, AF.Relu,
                  bias=l1col(4, o), reads=[dSM], writes=[dRW],
                  reads_bank=bank)
            KK.op("vector", "tensor_tensor",
                  STc(o)[:, tt * TW:(tt + 1) * TW], X, Yt, ALU.add,
                  reads=[dRW], writes=[dST])

        dense_fm(wdtT, D, B1c, dB1, evac_softplus)

        for wi, (wT, boff) in enumerate(((wBT, 0), (wCT, T))):
            ldw = K.dma("gpsimd", out=SMB[:, 0:NC * NST]
                        .rearrange("p (c g) -> p c g", c=NC),
                        in_=wT.rearrange("(c p) g -> p c g", p=128),
                        writes=[dSMB])
            for tt in range(NTT):
                bk = 6 + tt % 2
                chunks = [(SMB[:, k * NST:(k + 1) * NST],
                           B1c(k)[:, tt * TW:(tt + 1) * TW])
                          for k in range(NC)]
                mm_groups(K, [dict(bank=bk, ps=PS[bk][0:NST, 0:TW],
                                   chunks=chunks, reads=[dSMB, dB1],
                                   waits=[ldw])])
                K.op("scalar", "activation",
                     B2[0:NST, boff + tt * TW:boff + (tt + 1) * TW],
                     PS[bk][0:NST, 0:TW], AF.Identity,
                     bias=SM[0:NST, 336 + wi:337 + wi],
                     reads=[dSM], writes=[dB2], reads_bank=bk)


        def HCAR(ch):
            return SM[:, 370 + ch:371 + ch]

        def PACC(ch):
            return SM[:, 378 + ch:379 + ch]

        def yslice(ch, q):
            return F1[:, ch * SL + q * QW:ch * SL + (q + 1) * QW]

        dAp = [Dep(), Dep()]
        dBp = [Dep(), Dep()]
        dCp = [Dep(), Dep()]
        dDm = Dep()
        dCAR = Dep()
        dSUM = Dep()
        _bt2 = K.barrier_tokens()
        for _d in dAp + dBp + dCp + [dDm, dCAR, dSUM]:
            _d.r = list(_bt2)
        BQ2 = MSKB[:, 0:QW]
        CQ2 = MSKB[:, QW:2 * QW]

        def DMn(p):
            return B2[:, B2COLS - (2 + 2 * p) * QW:
                      B2COLS - (1 + 2 * p) * QW]

        def DMl(p):
            return B2[:, B2COLS - (1 + 2 * p) * QW:
                      B2COLS - 2 * p * QW]
        dDn = [Dep(), Dep()]
        dDl = [Dep(), Dep()]
        for _d in dDn + dDl:
            _d.r = list(_bt2)

        def slot(p, i):
            buf = RW if p == 0 else SCN
            return buf[:, i * QW:(i + 1) * QW]

        SDT8 = SM[:, 378:378 + NC]
        for b in range(B):
            K.op("vector", "memset", F1[:, 0:NC * SL], 0.0, writes=[dF1])
            for ch in range(NC):
                K.op("vector", "tensor_reduce", SM[:, 378 + ch:379 + ch],
                     STc(ch)[:, b * SL:(b + 1) * SL], AX.X, ALU.add,
                     reads=[dST], writes=[dSUM])
            PB = 404 + SUMW
            for n in range(NST):
                K.op("vector", "tensor_scalar",
                     SM[:, PB + n * NC:PB + (n + 1) * NC], SDT8,
                     acol(n), None, ALU.mult, reads=[dSUM, dSM],
                     writes=[dSUM])
            K.op("scalar", "activation", SM[:, PB:PB + SUMW],
                 SM[:, PB:PB + SUMW], AF.Exp, reads=[dSUM],
                 writes=[dSUM])
            for n in range(NST):
                for q in range(NQ):
                    csl = slice(b * SL + q * QW, b * SL + (q + 1) * QW)
                    K.mm(PS[2][:, 0:QW], SEL[0:NST, n * 128:(n + 1) * 128],
                         B2[0:NST, csl], 2, start=True, stop=True,
                         reads=[dB2, dSEL])
                    K.op("scalar", "activation", BQ2, PS[2][:, 0:QW],
                         AF.Identity, reads_bank=2, writes=[dMB])
                    K.mm(PS[3][:, 0:QW], SEL[0:NST, n * 128:(n + 1) * 128],
                         B2[0:NST, T + csl.start:T + csl.stop], 3,
                         start=True, stop=True, reads=[dB2, dSEL])
                    K.op("scalar", "activation", CQ2, PS[3][:, 0:QW],
                         AF.Identity, reads_bank=3, writes=[dMB])

                    def mstage1(ch, p):
                        Ap, Cp = slot(p, 0), slot(p, 2)
                        dts = STc(ch)[:, csl]
                        K.op("vector", "tensor_scalar", Ap, dts,
                             acol(n), -1e-38, ALU.mult, ALU.min,
                             reads=[dST, dSM], writes=[dAp[p]])
                        K.op("scalar", "activation", slot(p, 1), Ap,
                             AF.Exp, reads=[dAp[p]], writes=[dBp[p]])
                        K.op("scalar", "activation", Cp, Ap, AF.Ln,
                             scale=-1.0, reads=[dAp[p]],
                             writes=[dCp[p]])
                        K.op("scalar", "activation", Cp, Cp, AF.Exp,
                             scale=-1.0, reads=[dCp[p]], writes=[dCp[p]])

                    def mstage1b(ch, p):
                        Ap, Bp = slot(p, 0), slot(p, 1)
                        K.op("vector", "tensor_scalar", DMn(p), Bp,
                             -1.0, 1.0, ALU.mult, ALU.add,
                             reads=[dBp[p]], writes=[dDn[p]],
                             allow_lp=True)
                        K.op("vector", "tensor_scalar", DMl(p), Ap,
                             0.5, 1.0, ALU.mult, ALU.add,
                             reads=[dAp[p]], writes=[dDl[p]],
                             allow_lp=True)
                        K.op("vector", "tensor_scalar", Ap, Ap, -1e-4,
                             None, ALU.is_le, reads=[dAp[p]],
                             writes=[dAp[p]])

                    def mstage2(ch, p):
                        Ap, Bp, Cp = slot(p, 0), slot(p, 1), slot(p, 2)
                        xns = B1c(ch)[:, csl]
                        K.op("vector", "tensor_tensor", Cp, Cp, DMn(p),
                             ALU.mult, reads=[dDn[p], dCp[p]],
                             writes=[dCp[p]])
                        K.op("vector", "tensor_tensor", Cp, Cp, DMl(p),
                             ALU.subtract, reads=[dDl[p], dCp[p]],
                             writes=[dCp[p]])
                        K.op("vector", "tensor_tensor", Cp, Ap, Cp,
                             ALU.mult, reads=[dAp[p], dCp[p]],
                             writes=[dCp[p]])
                        K.op("vector", "tensor_tensor", Cp, Cp, DMl(p),
                             ALU.add, reads=[dDl[p], dCp[p]],
                             writes=[dCp[p]])
                        K.op("vector", "tensor_tensor", Cp, Cp, BQ2,
                             ALU.mult, reads=[dMB, dCp[p]],
                             writes=[dCp[p]])
                        K.op("vector", "tensor_tensor", Cp, Cp, xns,
                             ALU.mult, reads=[dB1, dCp[p]],
                             writes=[dCp[p]])
                        init = 0.0 if q == 0 else HCAR(ch)[:, 0:1]
                        rd = ([dBp[p], dCp[p]] if q == 0
                              else [dBp[p], dCp[p], dCAR])
                        K.op("vector", "tensor_tensor_scan", Ap, Bp,
                             Cp, init, ALU.mult, ALU.add, reads=rd,
                             writes=[dAp[p]])
                        if NQ > 1 and q < NQ - 1:
                            K.op("vector", "tensor_copy", HCAR(ch),
                                 Ap[:, QW - 1:QW], reads=[dAp[p]],
                                 writes=[dCAR])
                        if q == NQ - 1:
                            c_ = 404 + n * NC + ch
                            K.op("vector", "tensor_copy",
                                 SM[:, c_:c_ + 1], Ap[:, QW - 1:QW],
                                 reads=[dAp[p]], writes=[dSUM])
                        K.op("vector", "tensor_tensor", Cp, Ap, CQ2,
                             ALU.mult, reads=[dAp[p], dMB],
                             writes=[dCp[p]])
                        K.op("vector", "tensor_tensor", yslice(ch, q),
                             yslice(ch, q), Cp, ALU.add,
                             reads=[dCp[p], dF1], writes=[dF1])

                    prevc = None
                    for ch in range(NC):
                        mstage1(ch, ch % 2)
                        if prevc is not None:
                            mstage1b(prevc, (ch - 1) % 2)
                            mstage2(prevc, (ch - 1) % 2)
                        prevc = ch
                    mstage1b(prevc, (NC - 1) % 2)
                    mstage2(prevc, (NC - 1) % 2)
            FTMP = RW[:, 2 * QW:2 * QW + SL]
            dFT = [dCp[0]] if QW >= SL else [dCp[0], dAp[1]]
            for ch in range(NC):
                K.op("vector", "tensor_scalar", FTMP,
                     B1c(ch)[:, b * SL:(b + 1) * SL], l1col(2, ch), None,
                     ALU.mult, reads=[dB1, dSM], writes=dFT)
                K.op("vector", "tensor_tensor", F1[:, ch * SL:(ch + 1) * SL],
                     F1[:, ch * SL:(ch + 1) * SL], FTMP, ALU.add,
                     reads=dFT + [dF1], writes=[dF1])
                K.op("vector", "tensor_scalar", F1[:, ch * SL:(ch + 1) * SL],
                     F1[:, ch * SL:(ch + 1) * SL], l1col(3, ch), None,
                     ALU.mult, reads=[dSM, dF1], writes=[dF1])
            K.dma("sync", out=y_spill.rearrange("(c p) t -> p c t", p=128)
                  [:, :, b * SL:(b + 1) * SL],
                  in_=F1[:, 0:NC * SL].rearrange("p (c s) -> p c s", c=NC),
                  reads=[dF1], writes=[dYS])
            K.dma("sync", out=agm_in[b], in_=SM[:, 404:404 + 2 * SUMW],
                  reads=[dSUM], writes=[dAGM])

        agm_tok = K.coll("AllGather", ALU.bypass, [agm_in.opt()],
                         [agm_out.opt()], waits=K.barrier_tokens())
        dAGMo = Dep()
        dAGMo.did_write(agm_tok)

        STATE = SM[:, 404:404 + SUMW]
        Z0 = constcol(0.0)
        for b in range(B):
            tz = K.op("vector", "memset", STATE, 0.0,
                      writes=[dSM, dSUM])
            HPJ = F1[:, 0:2 * SUMW]
            for j in range(NCORES):
                ldj = K.dma("sync", out=HPJ, in_=agm_out[j, b],
                            reads=[dAGMo, dF1], writes=[dF1])
                K.op("vector", "tensor_scalar", HPJ[:, SUMW:2 * SUMW],
                     HPJ[:, SUMW:2 * SUMW], 1.0, SM[:, 396 + j:397 + j],
                     ALU.subtract, ALU.mult, reads=[dF1, dSM], writes=[dF1],
                     waits=[ldj])
                K.op("vector", "tensor_scalar", HPJ[:, SUMW:2 * SUMW],
                     HPJ[:, SUMW:2 * SUMW], 1.0, None, ALU.add,
                     reads=[dF1], writes=[dF1])
                K.op("vector", "tensor_scalar", HPJ[:, 0:SUMW],
                     HPJ[:, 0:SUMW], SM[:, 396 + j:397 + j], None,
                     ALU.mult, reads=[dF1, dSM], writes=[dF1])
                K.op("vector", "tensor_tensor", STATE, STATE,
                     HPJ[:, SUMW:2 * SUMW], ALU.mult, reads=[dF1, dSM],
                     writes=[dSM])
                K.op("vector", "tensor_tensor", STATE, STATE,
                     HPJ[:, 0:SUMW], ALU.add, reads=[dF1, dSM],
                     writes=[dSM])
            K.op("vector", "memset", F1[:, 0:NC * SL], 0.0, writes=[dF1])
            for n in range(NST):
                for q in range(NQ):
                    csl = slice(b * SL + q * QW, b * SL + (q + 1) * QW)
                    K.mm(PS[3][:, 0:QW], SEL[0:NST, n * 128:(n + 1) * 128],
                         B2[0:NST, T + csl.start:T + csl.stop], 3,
                         start=True, stop=True, reads=[dB2, dSEL])
                    K.op("scalar", "activation", CQ2, PS[3][:, 0:QW],
                         AF.Identity, reads_bank=3, writes=[dMB])

                    def cstage1(ch, p):
                        Ap = slot(p, 0)
                        K.op("vector", "tensor_scalar", Ap,
                             STc(ch)[:, csl], acol(n), -1e-38,
                             ALU.mult, ALU.min, reads=[dST, dSM],
                             writes=[dAp[p]])
                        K.op("scalar", "activation", slot(p, 1), Ap,
                             AF.Exp, reads=[dAp[p]], writes=[dBp[p]])

                    def cstage2(ch, p):
                        Ap, Bp, Cp = (slot(p, 0), slot(p, 1),
                                      slot(p, 2))
                        if q == 0:
                            init = STATE[:, n * NC + ch:
                                         n * NC + ch + 1]
                            rd = [dBp[p], dSM]
                        else:
                            init = HCAR(ch)[:, 0:1]
                            rd = [dBp[p], dCAR, dSM]
                        K.op("vector", "tensor_tensor_scan", Ap, Bp,
                             Z0.broadcast_to([128, QW]), init,
                             ALU.mult, ALU.add, reads=rd,
                             writes=[dAp[p]])
                        if NQ > 1 and q < NQ - 1:
                            K.op("vector", "tensor_copy", HCAR(ch),
                                 Ap[:, QW - 1:QW], reads=[dAp[p]],
                                 writes=[dCAR])
                        K.op("vector", "tensor_tensor", Cp, Ap, CQ2,
                             ALU.mult, reads=[dAp[p], dMB],
                             writes=[dCp[p]])
                        K.op("vector", "tensor_tensor", yslice(ch, q),
                             yslice(ch, q), Cp, ALU.add,
                             reads=[dCp[p], dF1], writes=[dF1])

                    prevc = None
                    for ch in range(NC):
                        cstage1(ch, ch % 2)
                        if prevc is not None:
                            cstage2(prevc, (ch - 1) % 2)
                        prevc = ch
                    cstage2(prevc, (NC - 1) % 2)
            for ch in range(NC):
                K.op("vector", "tensor_scalar", F1[:, ch * SL:(ch + 1) * SL],
                     F1[:, ch * SL:(ch + 1) * SL], l1col(3, ch), None,
                     ALU.mult, reads=[dSM, dF1], writes=[dF1])
            K.dma("gpsimd", out=y_spill.rearrange("(c p) t -> p c t", p=128)
                  [:, :, b * SL:(b + 1) * SL],
                  in_=F1[:, 0:NC * SL].rearrange("p (c s) -> p c s", c=NC),
                  reads=[dF1, dYS], writes=[dYS], accum_op=ALU.add)

        t1 = K.dma("sync", out=ST[:].rearrange("p (c t) -> p c t", c=NC),
                   in_=st_spill.rearrange("(c p) t -> p c t", p=128),
                   reads=[dSP], writes=[dST], waits=K.barrier_tokens())
        t2 = K.dma("gpsimd", out=ST[:].rearrange("p (c t) -> p c t", c=NC),
                   in_=y_spill.rearrange("(c p) t -> p c t", p=128),
                   reads=[dYS, dST], writes=[dST], accum_op=ALU.add,
                   waits=[t1])
        K.wait("gpsimd", [t2])

        phase_barrier()
        layernorm(STc, dST, B1c, dB1, gamma_col=lambda k: l1col(1, k),
                  rms=True, eps=EPS_RMS)

        HSW2 = min(1024, DFF)
        NHS2 = DFF // HSW2
        HCS2 = HSW2 // 128
        for hs in range(NHS2):
            wbuf, wdep = (WA, dWA) if hs % 2 == 0 else (WBU, dWB)
            ldw1 = K.dma("gpsimd",
                         out=wbuf[:, 0:NC * HSW2]
                         .rearrange("p (c h) -> p c h", c=NC),
                         in_=w1lT[:, hs * HSW2:(hs + 1) * HSW2]
                         .rearrange("(c p) h -> p c h", p=128),
                         writes=[wdep])
            pend = []
            for hc in range(HCS2):
                for tt in range(NTT):
                    bank = (hc * NTT + tt) % 4
                    chunks = [(wbuf[:, k * HSW2 + hc * 128:
                                    k * HSW2 + hc * 128 + 128],
                               B1c(k)[:, tt * TW:(tt + 1) * TW])
                              for k in range(NC)]

                    def ev_h2(KK, hc=hc, tt=tt, bank=bank, hs=hs):
                        KK.op("scalar", "activation",
                              B2[:, hc * T + tt * TW:hc * T + (tt + 1) * TW],
                              PS[bank][:, 0:TW], AF.Gelu_apprx_tanh,
                              bias=b1lcol(hs * HCS2 + hc),
                              reads=[dSM], writes=[dB2], reads_bank=bank)
                    pend.append(dict(bank=bank, ps=PS[bank][:, 0:TW],
                                     chunks=chunks, reads=[wdep, dB1],
                                     waits=[ldw1], evac=ev_h2))
                    if len(pend) == 4:
                        mm_groups(K, pend)
                        pend = []
            if pend:
                mm_groups(K, pend)
            wb2, wd2 = (WBU, dWB) if hs % 2 == 0 else (WA, dWA)
            ldw2 = K.dma("gpsimd",
                         out=wb2[:, 0:HCS2 * D]
                         .rearrange("p (c o) -> p c o", c=HCS2),
                         in_=w2lT[hs * HSW2:(hs + 1) * HSW2, :]
                         .rearrange("(c p) o -> p c o", p=128),
                         writes=[wd2])
            pend = []
            for tt in range(NTT):
                for oc in range(NC):
                    bank = 4 + (oc + tt) % 2
                    chunks = [(wb2[:, c * D + oc * 128:
                                   c * D + oc * 128 + 128],
                               B2[:, c * T + tt * TW:c * T + (tt + 1) * TW])
                              for c in range(HCS2)]

                    def ev_o2(KK, oc=oc, tt=tt, bank=bank, hs=hs):
                        if hs == 0:
                            KK.op("vector", "scalar_tensor_tensor",
                                  STc(oc)[:, tt * TW:(tt + 1) * TW],
                                  PS[bank][:, 0:TW], l1col(5, oc),
                                  STc(oc)[:, tt * TW:(tt + 1) * TW],
                                  ALU.add, ALU.add,
                                  reads=[dSM, dST], writes=[dST],
                                  reads_bank=bank)
                        else:
                            KK.op("vector", "tensor_tensor",
                                  STc(oc)[:, tt * TW:(tt + 1) * TW],
                                  STc(oc)[:, tt * TW:(tt + 1) * TW],
                                  PS[bank][:, 0:TW], ALU.add,
                                  reads=[dST], writes=[dST],
                                  reads_bank=bank)
                    pend.append(dict(bank=bank, ps=PS[bank][:, 0:TW],
                                     chunks=chunks, reads=[wd2, dB2],
                                     waits=[ldw2], evac=ev_o2))
                    if len(pend) == 2:
                        mm_groups(K, pend)
                        pend = []
            if pend:
                mm_groups(K, pend)

    if upto >= 6:
        wgq = din("wgq", [GI, D])
        wuq = din("wuq", [GI, D])
        wdq = din("wdq", [D, GI])
        dqgT = dint("dqgT", [D * GI], BF16).rearrange("(a b) -> a b", a=D)
        dquT = dint("dquT", [D * GI], BF16).rearrange("(a b) -> a b", a=D)
        dqdT = dint("dqdT", [GI * D], BF16).rearrange("(a b) -> a b", a=GI)
        g_raw = dint("g_raw", [GI * T], BF16).rearrange("(a b) -> a b", a=GI)
        u_raw = dint("u_raw", [GI * T], BF16).rearrange("(a b) -> a b", a=GI)
        g2_raw = dint("g2_raw", [GI * T], BF16).rearrange("(a b) -> a b",
                                                          a=GI)
        uq_st = dint("uq_st", [GI * T], BF16).rearrange("(a b) -> a b", a=GI)
        oq_st = dint("oq_st", [GI * T], BF16).rearrange("(a b) -> a b", a=GI)
        amax_in = dint("amax_in", [16])
        amax_out = dint("amax_out", [NCORES * 16])
        dDQ = Dep()
        dGRg = Dep()
        dGRu = Dep()
        dGRg2 = Dep()
        dGRuq = Dep()
        dGRoq = Dep()
        dAMX = Dep()
        NGC = GI // 128

        phase_barrier()
        layernorm(STc, dST, B1c, dB1)

        dITMs = {}
        dAM = Dep()

        def round_int(ap_f32, n, dep=None, eng="vector", isl=None,
                      ibuf=None):
            dep = dep or dF1
            base, cw = isl if isl is not None else (0, 512)
            ib = ibuf if ibuf is not None else ITMP
            key = (id(ib), base, cw)
            if key not in dITMs:
                dITMs[key] = Dep()
                dITMs[key].r = K.barrier_tokens()
            dI = dITMs[key]
            for s0 in range(0, n, cw):
                w = min(cw, n - s0)
                K.op(eng, "tensor_copy", ib[:, base:base + w],
                     ap_f32[:, s0:s0 + w], reads=[dep], writes=[dI])
                K.op(eng, "tensor_copy", ap_f32[:, s0:s0 + w],
                     ib[:, base:base + w], reads=[dI], writes=[dep])

        def dequant_transpose(w_dram, dqT_dram, O, I):
            NB = I // 32
            for r in range(O // 128):
                ldw = K.dma("sync", out=F1[:, 0:I],
                            in_=w_dram[r * 128:(r + 1) * 128, :],
                            writes=[dF1])
                K.op("vector", "tensor_reduce", RW[:, 0:NB],
                     F1[:, 0:I].rearrange("p (nb w) -> p nb w", w=32),
                     AX.X, ALU.max, apply_absolute_value=True,
                     reads=[dF1], writes=[dRW], waits=[ldw])
                K.op("vector", "tensor_scalar", RW[:, NB:2 * NB],
                     RW[:, 0:NB], 1.0 / 7.0, 1e-12, ALU.mult, ALU.max,
                     reads=[dRW], writes=[dRW])
                K.op("vector", "reciprocal", RW[:, 2 * NB:3 * NB],
                     RW[:, NB:2 * NB], reads=[dRW], writes=[dRW])
                K.op("vector", "tensor_tensor", F1[:, 0:I]
                     .rearrange("p (nb w) -> p nb w", w=32),
                     F1[:, 0:I].rearrange("p (nb w) -> p nb w", w=32),
                     RW[:, 2 * NB:3 * NB].unsqueeze(2)
                     .broadcast_to([128, NB, 32]), ALU.mult,
                     reads=[dRW], writes=[dF1])
                round_int(F1[:, 0:I], I)
                K.op("vector", "tensor_scalar", F1[:, 0:I], F1[:, 0:I],
                     7.0, -8.0, ALU.min, ALU.max, reads=[dF1], writes=[dF1])
                K.op("vector", "tensor_tensor", F1[:, 0:I]
                     .rearrange("p (nb w) -> p nb w", w=32),
                     F1[:, 0:I].rearrange("p (nb w) -> p nb w", w=32),
                     RW[:, NB:2 * NB].unsqueeze(2)
                     .broadcast_to([128, NB, 32]), ALU.mult,
                     reads=[dRW], writes=[dF1])
                for ic in range(I // 128):
                    tb = 2 + ic % 2
                    K.transpose(PS[tb][:, 0:128],
                                F1[:, ic * 128:(ic + 1) * 128], IDENT, tb,
                                reads=[dF1, dSM])
                    K.op("vector", "tensor_copy",
                         B2[:, ic * 128:(ic + 1) * 128], PS[tb][:, 0:128],
                         reads_bank=tb, writes=[dB2])
                K.dma("sync",
                      out=dqT_dram[:, r * 128:(r + 1) * 128]
                      .rearrange("(ic p) o -> p ic o", p=128),
                      in_=B2[:, 0:I].rearrange("p (ic o) -> p ic o", o=128),
                      reads=[dB2], writes=[dDQ])

        dequant_transpose(wgq, dqgT, GI, D)
        dequant_transpose(wuq, dquT, GI, D)
        dequant_transpose(wdq, dqdT, D, GI)

        AMG = SM[:, 386:387]
        AMU = SM[:, 387:388]
        for k in range(NC):
            K.op("vector", "tensor_reduce", AMU, B1c(k), AX.X, ALU.max,
                 apply_absolute_value=True, reads=[dB1], writes=[dAM])
            if k == 0:
                K.op("vector", "tensor_copy", AMG, AMU, reads=[dAM],
                     writes=[dAM])
            else:
                K.op("vector", "tensor_tensor", AMG, AMG, AMU, ALU.max,
                     reads=[dAM], writes=[dAM])

        def part_reduce_max(col_ap, dst_row_ap, bank):
            K.transpose(PS[bank][0:1, 0:128], col_ap, IDENT, bank,
                        reads=[dAM, dSM])
            K.op("vector", "tensor_reduce", dst_row_ap, PS[bank][0:1, 0:128],
                 AX.X, ALU.max, reads_bank=bank, writes=[dAM])

        def qdq_cols(scale_row, rs_col, s_col, bank):
            K.op("vector", "tensor_scalar", scale_row, scale_row,
                 1.0 / 32767.0, 1e-12, ALU.mult, ALU.max, reads=[dAM],
                 writes=[dAM])
            K.mm(PS[bank][:, 0:1], ONES[0:1, 0:128], scale_row, bank,
                 start=True, stop=True, reads=[dAM, dSM])
            K.op("vector", "tensor_copy", s_col, PS[bank][:, 0:1],
                 reads_bank=bank, writes=[dSM])
            K.op("vector", "reciprocal", rs_col, s_col, reads=[dSM],
                 writes=[dSM])

        XMROW = SM[0:1, 440:456]
        K.op("vector", "memset", XMROW, 0.0, writes=[dAM])

        def set_xmrow():
            K.op("vector", "tensor_copy", SM[0:1, 440:442],
                 SM[0:1, 386:388], reads=[dAM], writes=[dAM])
        part_reduce_max(AMG, SM[0:1, 386:387], 2)
        set_xmrow()
        t_am = K.dma("sync", out=amax_in, in_=XMROW, reads=[dAM],
                     writes=[dAMX])
        ag1 = K.coll("AllGather", ALU.bypass, [amax_in.opt()],
                     [amax_out.opt()], waits=[t_am] + K.barrier_tokens())
        dAMXo = Dep()
        dAMXo.did_write(ag1)
        t_l = K.dma("sync", out=SM[0:1, 400:400 + NCORES],
                    in_=amax_out.rearrange("(r c) -> r c", c=16)[:, 0]
                    .rearrange("(j r) -> j r", j=1),
                    reads=[dAMXo], writes=[dAM],
                    allow_slow_non_contiguous=True)
        K.op("vector", "tensor_reduce", SM[0:1, 386:387],
             SM[0:1, 400:400 + NCORES],
             AX.X, ALU.max, reads=[dAM], writes=[dAM], waits=[t_l])
        RS_X = SM[:, 408:409]
        S_X = SM[:, 409:410]
        qdq_cols(SM[0:1, 386:387], RS_X, S_X, 2)
        for k in range(NC):
            for tt in range(NTT):
                sl_ = slice(tt * TW, (tt + 1) * TW)
                K.op("vector", "tensor_scalar", F1[:, 0:TW], B1c(k)[:, sl_],
                     RS_X, None, ALU.mult, reads=[dB1, dSM], writes=[dF1])
                round_int(F1[:, 0:TW], TW)
                K.op("vector", "tensor_scalar", F1[:, 0:TW], F1[:, 0:TW],
                     32767.0, -32768.0, ALU.min, ALU.max, reads=[dF1],
                     writes=[dF1])
                K.op("vector", "tensor_scalar",
                     B2c(k)[:, sl_] if False else
                     B2[:, k * T + sl_.start:k * T + sl_.stop],
                     F1[:, 0:TW], S_X, None, ALU.mult,
                     reads=[dF1, dSM], writes=[dB2])

        AMG2 = SM[:, 410:411]
        first_flag = [True]

        def raw_mm(dqT, out_dram, amax_col, outdep):
            first = [True]
            HWO = min(512, GI)
            for half in range(GI // HWO):
                wb_, wd_ = (WA, dWA) if half % 2 == 0 else (WBU, dWB)
                ldw = K.dma("sync", out=wb_[:, 0:NC * HWO]
                            .rearrange("p (c o) -> p c o", c=NC),
                            in_=dqT[:, half * HWO:(half + 1) * HWO]
                            .rearrange("(c p) o -> p c o", p=128),
                            reads=[dDQ], writes=[wd_])
                pend = []
                for oc in range(HWO // 128):
                    og = half * (HWO // 128) + oc
                    for tt in range(NTT):
                        bank = (oc * NTT + tt) % 4
                        chunks = [(wb_[:, k * HWO + oc * 128:
                                       k * HWO + oc * 128 + 128],
                                   B2[:, k * T + tt * TW:
                                      k * T + (tt + 1) * TW])
                                  for k in range(NC)]

                        def ev(KK, og=og, tt=tt, bank=bank, first=first):
                            KK.op("vector", "tensor_copy", MSKB[:, 0:TW],
                                  PS[bank][:, 0:TW], reads_bank=bank,
                                  writes=[dMB])
                            KK.dma("gpsimd",
                                   out=out_dram[og * 128:(og + 1) * 128,
                                                tt * TW:(tt + 1) * TW],
                                   in_=MSKB[:, 0:TW], reads=[dMB],
                                   writes=[outdep])
                            KK.op("vector", "tensor_reduce", AMU,
                                  PS[bank][:, 0:TW], AX.X, ALU.max,
                                  apply_absolute_value=True,
                                  reads_bank=bank, writes=[dAM])
                            if first[0]:
                                KK.op("vector", "tensor_copy", amax_col,
                                      AMU, reads=[dAM], writes=[dAM])
                                first[0] = False
                            else:
                                KK.op("vector", "tensor_tensor", amax_col,
                                      amax_col, AMU, ALU.max, reads=[dAM],
                                      writes=[dAM])
                        pend.append(dict(bank=bank, ps=PS[bank][:, 0:TW],
                                         chunks=chunks,
                                         reads=[wd_, dB2], waits=[ldw],
                                         evac=ev))
                        if len(pend) == 4:
                            mm_groups(K, pend)
                            pend = []
                if pend:
                    mm_groups(K, pend)

        raw_mm(dqgT, g_raw, AMG, dGRg)
        raw_mm(dquT, u_raw, AMG2, dGRu)
        part_reduce_max(AMG, SM[0:1, 386:387], 2)
        part_reduce_max(AMG2, SM[0:1, 387:388], 3)
        set_xmrow()
        t_am = K.dma("sync", out=amax_in, in_=XMROW, reads=[dAM],
                     writes=[dAMX], waits=dAMXo.for_write())
        ag2 = K.coll("AllGather", ALU.bypass, [amax_in.opt()],
                     [amax_out.opt()], waits=[t_am] + K.barrier_tokens())
        dAMXo.did_write(ag2)
        t_l = K.dma("sync", out=SM[0:1, 400:400 + NCORES],
                    in_=amax_out.rearrange("(r c) -> r c", c=16)[:, 0]
                    .rearrange("(j r) -> j r", j=1),
                    reads=[dAMXo], writes=[dAM],
                    allow_slow_non_contiguous=True)
        t_l2 = K.dma("sync", out=SM[0:1, 424:424 + NCORES],
                     in_=amax_out.rearrange("(r c) -> r c", c=16)[:, 1]
                     .rearrange("(j r) -> j r", j=1),
                     reads=[dAMXo], writes=[dAM],
                     allow_slow_non_contiguous=True)
        K.op("vector", "tensor_reduce", SM[0:1, 386:387],
             SM[0:1, 400:400 + NCORES],
             AX.X, ALU.max, reads=[dAM], writes=[dAM], waits=[t_l])
        K.op("vector", "tensor_reduce", SM[0:1, 387:388],
             SM[0:1, 424:424 + NCORES],
             AX.X, ALU.max, reads=[dAM], writes=[dAM], waits=[t_l2])
        RS_G = SM[:, 411:412]
        S_G = SM[:, 412:413]
        RS_U = SM[:, 413:414]
        S_U = SM[:, 414:415]
        qdq_cols(SM[0:1, 386:387], RS_G, S_G, 2)
        qdq_cols(SM[1:2, 386:387] if False else SM[0:1, 387:388],
                 RS_U, S_U, 3)

        def qdq_tile(ap, rs_col, s_col, n, dep=None, eng="vector",
                     isl=None, src=None, srcreads=(), ibuf=None):
            dep = dep or dF1
            if src is None:
                K.op(eng, "tensor_scalar", ap, ap, rs_col, None, ALU.mult,
                     reads=[dSM, dep], writes=[dep])
            else:
                K.op(eng, "tensor_scalar", ap, src, rs_col, None, ALU.mult,
                     reads=[dSM] + list(srcreads), writes=[dep])
            round_int(ap, n, dep, eng, isl, ibuf)
            K.op(eng, "tensor_scalar", ap, ap, 32767.0, -32768.0,
                 ALU.min, ALU.max, reads=[dep], writes=[dep])
            K.op(eng, "tensor_scalar", ap, ap, s_col, None, ALU.mult,
                 reads=[dSM, dep], writes=[dep])

        AM3 = SM[:, 415:416]
        firstA = [True]
        NPAR = 4
        dFp = [Dep() for _ in range(NPAR)]
        dB1p = [Dep() for _ in range(NPAR)]
        _bt4 = K.barrier_tokens()
        for _d in dFp + dB1p:
            _d.r = list(_bt4)

        def ptile(up_):
            FA = F1[:, up_ * 2 * TW:up_ * 2 * TW + TW]
            FB = F1[:, up_ * 2 * TW + TW:up_ * 2 * TW + 2 * TW]
            GL = B1[:, (up_ * 3) * TW:(up_ * 3 + 1) * TW]
            UL = B1[:, (up_ * 3 + 1) * TW:(up_ * 3 + 2) * TW]
            UQ = B1[:, (up_ * 3 + 2) * TW:(up_ * 3 + 3) * TW]
            ib = ITMP if up_ < 2 else ITMP2
            return (FA, FB, GL, UL, UQ, dFp[up_], dB1p[up_],
                    ((up_ % 2) * 256, 256), ib)

        for og in range(NGC):
            for tt in range(NTT):
                up_ = (og * NTT + tt) % NPAR
                FA, FB, GL, UL, UQ, dFx, dBx, isl_, ib_ = ptile(up_)
                r0 = slice(og * 128, (og + 1) * 128)
                c0 = slice(tt * TW, (tt + 1) * TW)
                ldg = K.dma("sync", out=GL, in_=g_raw[r0, c0],
                            reads=[dGRg], writes=[dBx])
                ldu = K.dma("scalar", out=UL, in_=u_raw[r0, c0],
                            reads=[dGRu], writes=[dBx])
                qdq_tile(FA, RS_G, S_G, TW, dFx, isl=isl_,
                         src=GL, srcreads=[dBx, dFx], ibuf=ib_)
                K.op("scalar", "activation", FB, FA, AF.Sigmoid,
                     reads=[dFx], writes=[dFx])
                K.op("vector", "tensor_scalar", FB, FB, 65536.0, None,
                     ALU.mult, reads=[dFx], writes=[dFx])
                round_int(FB, TW, dFx, isl=isl_, ibuf=ib_)
                K.op("vector", "tensor_scalar", FB, FB, 65535.0, 0.0,
                     ALU.min, ALU.max, reads=[dFx], writes=[dFx])
                K.op("vector", "tensor_scalar", FB, FB, 1.0 / 65536.0,
                     None, ALU.mult, reads=[dFx], writes=[dFx])
                K.op("vector", "tensor_tensor", FA, FA, FB, ALU.mult,
                     reads=[dFx], writes=[dFx])
                K.dma("gpsimd", out=g2_raw[r0, c0], in_=FA,
                      reads=[dFx], writes=[dGRg2])
                K.op("vector", "tensor_reduce", AMU, FA, AX.X, ALU.max,
                     apply_absolute_value=True, reads=[dFx], writes=[dAM])
                if firstA[0]:
                    K.op("vector", "tensor_copy", AM3, AMU, reads=[dAM],
                         writes=[dAM])
                    firstA[0] = False
                else:
                    K.op("vector", "tensor_tensor", AM3, AM3, AMU,
                         ALU.max, reads=[dAM], writes=[dAM])
                K.op("vector", "tensor_scalar", FB, UL, RS_U, None,
                     ALU.mult, reads=[dSM, dBx, dFx], writes=[dFx])
                round_int(FB, TW, dFx, eng="vector", isl=isl_, ibuf=ib_)
                K.op("vector", "tensor_scalar", FB, FB, 32767.0,
                     -32768.0, ALU.min, ALU.max, reads=[dFx],
                     writes=[dFx])
                K.op("vector", "tensor_scalar", UQ, FB, S_U, None,
                     ALU.mult, reads=[dSM, dFx], writes=[dBx],
                     allow_lp=True)
                K.dma("sync", out=uq_st[r0, c0], in_=UQ, reads=[dBx],
                      writes=[dGRuq])

        part_reduce_max(AM3, SM[0:1, 386:387], 2)
        set_xmrow()
        t_am = K.dma("sync", out=amax_in, in_=XMROW, reads=[dAM],
                     writes=[dAMX], waits=dAMXo.for_write())
        ag3 = K.coll("AllGather", ALU.bypass, [amax_in.opt()],
                     [amax_out.opt()], waits=[t_am] + K.barrier_tokens())
        dAMXo.did_write(ag3)
        t_l = K.dma("sync", out=SM[0:1, 400:400 + NCORES],
                    in_=amax_out.rearrange("(r c) -> r c", c=16)[:, 0]
                    .rearrange("(j r) -> j r", j=1),
                    reads=[dAMXo], writes=[dAM],
                    allow_slow_non_contiguous=True)
        K.op("vector", "tensor_reduce", SM[0:1, 386:387],
             SM[0:1, 400:400 + NCORES],
             AX.X, ALU.max, reads=[dAM], writes=[dAM], waits=[t_l])
        RS_2 = SM[:, 416:417]
        S_2 = SM[:, 417:418]
        qdq_cols(SM[0:1, 386:387], RS_2, S_2, 2)

        AM4 = SM[:, 418:419]
        firstB = [True]
        for og in range(NGC):
            for tt in range(NTT):
                up_ = (og * NTT + tt) % NPAR
                FA, FB, GL, UL, UQ, dFx, dBx, isl_, ib_ = ptile(up_)
                r0 = slice(og * 128, (og + 1) * 128)
                c0 = slice(tt * TW, (tt + 1) * TW)
                K.dma("sync", out=GL, in_=g2_raw[r0, c0], reads=[dGRg2],
                      writes=[dBx])
                K.dma("scalar", out=UL, in_=uq_st[r0, c0], reads=[dGRuq],
                      writes=[dBx])
                qdq_tile(FA, RS_2, S_2, TW, dFx, isl=isl_,
                         src=GL, srcreads=[dBx, dFx], ibuf=ib_)
                K.op("vector", "tensor_tensor", FA, FA, UL, ALU.mult,
                     reads=[dBx, dFx], writes=[dFx])
                K.dma("gpsimd", out=oq_st[r0, c0], in_=FA, reads=[dFx],
                      writes=[dGRoq])
                K.op("vector", "tensor_reduce", AMU, FA, AX.X, ALU.max,
                     apply_absolute_value=True, reads=[dFx], writes=[dAM])
                if firstB[0]:
                    K.op("vector", "tensor_copy", AM4, AMU, reads=[dAM],
                         writes=[dAM])
                    firstB[0] = False
                else:
                    K.op("vector", "tensor_tensor", AM4, AM4, AMU,
                         ALU.max, reads=[dAM], writes=[dAM])

        part_reduce_max(AM4, SM[0:1, 386:387], 2)
        set_xmrow()
        t_am = K.dma("sync", out=amax_in, in_=XMROW, reads=[dAM],
                     writes=[dAMX], waits=dAMXo.for_write())
        ag4 = K.coll("AllGather", ALU.bypass, [amax_in.opt()],
                     [amax_out.opt()], waits=[t_am] + K.barrier_tokens())
        dAMXo.did_write(ag4)
        t_l = K.dma("sync", out=SM[0:1, 400:400 + NCORES],
                    in_=amax_out.rearrange("(r c) -> r c", c=16)[:, 0]
                    .rearrange("(j r) -> j r", j=1),
                    reads=[dAMXo], writes=[dAM],
                    allow_slow_non_contiguous=True)
        K.op("vector", "tensor_reduce", SM[0:1, 386:387],
             SM[0:1, 400:400 + NCORES],
             AX.X, ALU.max, reads=[dAM], writes=[dAM], waits=[t_l])
        RS_O = SM[:, 419:420]
        S_O = SM[:, 420:421]
        qdq_cols(SM[0:1, 386:387], RS_O, S_O, 2)

        for tt in range(NTT):
            hcn = NGC // 2
            ldo = K.dma("sync", out=B1[:, 0:hcn * TW]
                        .rearrange("p (c w) -> p c w", c=hcn),
                        in_=oq_st.rearrange("(c p) t -> p c t", p=128)
                        [:, 0:hcn, tt * TW:(tt + 1) * TW],
                        reads=[dGRoq], writes=[dB1])
            ldo2 = K.dma("gpsimd", out=B1[:, hcn * TW:NGC * TW]
                         .rearrange("p (c w) -> p c w", c=NGC - hcn),
                         in_=oq_st.rearrange("(c p) t -> p c t", p=128)
                         [:, hcn:NGC, tt * TW:(tt + 1) * TW],
                         reads=[dGRoq], writes=[dB1])
            for c_ in range(NGC):
                FQ = F1[:, (c_ % 8) * TW:(c_ % 8 + 1) * TW]
                dFx = dFp[(c_ % 8) // 2]
                wt = [ldo if c_ < hcn else ldo2]
                K.op("vector", "tensor_scalar", FQ, B1[:, c_ * TW:
                     (c_ + 1) * TW], RS_O, None, ALU.mult,
                     reads=[dSM, dB1, dFx], writes=[dFx], waits=wt)
                round_int(FQ, TW, dFx, isl=((c_ % 2) * 256, 256))
                K.op("vector", "tensor_scalar", FQ, FQ, 32767.0,
                     -32768.0, ALU.min, ALU.max, reads=[dFx],
                     writes=[dFx])
                K.op("vector", "tensor_scalar", B1[:, c_ * TW:
                     (c_ + 1) * TW], FQ, S_O, None, ALU.mult,
                     reads=[dSM, dFx], writes=[dB1], allow_lp=True)
            for grp in range(D // 256):
                wb_, wd_ = (WA, dWA) if grp % 2 == 0 else (WBU, dWB)
                wwt = K.barrier_tokens() if tt == 0 and grp < 2 else ()
                ldw = K.dma("sync", out=wb_[:, 0:NGC * 256]
                            .rearrange("p (c o) -> p c o", c=NGC),
                            in_=dqdT[:, grp * 256:(grp + 1) * 256]
                            .rearrange("(c p) o -> p c o", p=128),
                            reads=[dDQ], writes=[wd_], waits=wwt)
                pend = []
                for oc2 in range(2):
                    oc = grp * 2 + oc2
                    bank = oc2 % 2
                    chunks = [(wb_[:, c * 256 + oc2 * 128:
                                   c * 256 + oc2 * 128 + 128],
                               B1[:, c * TW:(c + 1) * TW])
                              for c in range(NGC)]

                    def ev_d(KK, oc=oc, tt=tt, bank=bank):
                        KK.op("vector", "tensor_tensor",
                              STc(oc)[:, tt * TW:(tt + 1) * TW],
                              STc(oc)[:, tt * TW:(tt + 1) * TW],
                              PS[bank][:, 0:TW], ALU.add,
                              reads=[dST], writes=[dST], reads_bank=bank)
                    pend.append(dict(bank=bank, ps=PS[bank][:, 0:TW],
                                     chunks=chunks, reads=[wd_, dB1],
                                     waits=[ldw, ldo], evac=ev_d))
                mm_groups(K, pend)

    if dbg and upto >= 4:
        dbg_msks = nc.dram_tensor("dbg_msks", [9, T], BF16,
                                  kind="ExternalOutput").ap()
        dbg_xrt = nc.dram_tensor("dbg_xrt", [NST, T], F32,
                                 kind="ExternalOutput").ap()
        dbg_xn = nc.dram_tensor("dbg_xn", [D, T], BF16,
                                kind="ExternalOutput").ap()
        K.dma("sync", out=dbg_msks, in_=MSKS[:, :],
              waits=K.barrier_tokens())
        K.dma("sync", out=dbg_xrt, in_=XRT[:, :], waits=K.barrier_tokens())
        K.dma("sync", out=dbg_xn.rearrange("(c p) t -> p c t", p=128),
              in_=B1[:, 0:NC * T].rearrange("p (c t) -> p c t", c=NC),
              waits=K.barrier_tokens())

    st_tok = K.dma("sync", out=xout.rearrange("(c p) t -> p c t", p=128),
                   in_=ST[:].rearrange("p (c t) -> p c t", c=NC),
                   reads=[dST], waits=K.barrier_tokens())
    K.wait("sync", [st_tok])

    with ExitStack() as ctx:
        K.build_block(ctx)
    return nc



def _shard_inputs(inputs):
    f32 = np.float32
    S = SL * NCORES
    T = B * SL
    x = np.asarray(inputs["L0_attn_x"], f32)
    wqT = np.ascontiguousarray(np.asarray(inputs["L0_attn_Wq"], f32).T)
    wkT = np.ascontiguousarray(np.asarray(inputs["L0_attn_Wk"], f32).T)
    wvT = np.ascontiguousarray(np.asarray(inputs["L0_attn_Wv"], f32).T)
    woT = np.ascontiguousarray(np.asarray(inputs["L0_attn_Wo"], f32).T)
    bqk = np.ascontiguousarray(np.stack([
        np.asarray(inputs["L0_attn_bq"], f32),
        np.asarray(inputs["L0_attn_bk"], f32),
        np.asarray(inputs["L0_attn_bv"], f32),
        np.asarray(inputs["L0_attn_bo"], f32)]))
    com = dict(wqT=wqT, wkT=wkT, wvT=wvT, woT=woT, bqk=bqk)
    if "L1_attn_Wdt" in inputs:
        g = lambda k: np.asarray(inputs[k], f32)
        com.update(
            wdtT=np.ascontiguousarray(g("L1_attn_Wdt").T),
            wBT=np.ascontiguousarray(g("L1_attn_WB").T),
            wCT=np.ascontiguousarray(g("L1_attn_WC").T),
            w1lT=np.ascontiguousarray(g("L1_attn_W1").T),
            w2lT=np.ascontiguousarray(g("L1_attn_W2").T),
            l1vec=np.ascontiguousarray(np.stack([
                g("L1_attn_norm1_w"), g("L1_attn_norm2_w"),
                g("L1_attn_D_param"), g("L1_attn_scale"),
                g("L1_attn_bdt"), g("L1_attn_b2")])),
            l1b1=np.ascontiguousarray(g("L1_attn_b1")),
            l1bBC=np.ascontiguousarray(np.stack([g("L1_attn_bB"),
                                                 g("L1_attn_bC")])),
            l1A=np.ascontiguousarray(g("L1_attn_A_log")),
        )
    if "L1_ffn_w_gate" in inputs:
        com.update(
            wgq=np.ascontiguousarray(np.asarray(inputs["L1_ffn_w_gate"],
                                                f32)),
            wuq=np.ascontiguousarray(np.asarray(inputs["L1_ffn_w_up"], f32)),
            wdq=np.ascontiguousarray(np.asarray(inputs["L1_ffn_w_down"],
                                                f32)),
        )
    if "L0_ffn_w1" in inputs:
        com.update(
            moe_w1=np.ascontiguousarray(np.asarray(inputs["L0_ffn_w1"], f32)),
            moe_w2=np.ascontiguousarray(np.asarray(inputs["L0_ffn_w2"], f32)),
            moe_b1=np.ascontiguousarray(np.asarray(inputs["L0_ffn_b1"], f32)),
            moe_b2=np.ascontiguousarray(np.asarray(inputs["L0_ffn_b2"], f32)),
            wgrT=np.ascontiguousarray(
                np.asarray(inputs["L0_ffn_wg_red"], f32).T),
            wg=np.ascontiguousarray(np.asarray(inputs["L0_ffn_wg"], f32)),
        )
    maps = []
    for i in range(NCORES):
        sl = x[:, i * SL:(i + 1) * SL, :].reshape(T, D)
        m = dict(xin=np.ascontiguousarray(sl.T), **com)
        if "L1_attn_Wdt" in inputs:
            m["cmask"] = (np.arange(8) < i).astype(f32)
        maps.append(m)
    return maps


def kernel(**inputs):
    S = SL * NCORES
    nc = build_program()
    in_maps = _shard_inputs(inputs)
    res = run_bass_kernel_spmd(nc, in_maps, core_ids=list(range(NCORES)))
    out = np.empty((B, S, D), np.float32)
    for i in range(NCORES):
        sh = np.asarray(res.results[i]["xout"])
        out[:, i * SL:(i + 1) * SL, :] = sh.T.reshape(B, SL, D)
    return out
